# version 25
# speedup vs baseline: 1.2002x; 1.0017x over previous
import numpy as np
import ml_dtypes
from contextlib import ExitStack

import concourse.bass as bass
import concourse.mybir as mybir
from concourse.bass_utils import run_bass_kernel_spmd

F32 = mybir.dt.float32
BF16 = mybir.dt.bfloat16
I32 = mybir.dt.int32
AX = mybir.AxisListType.X
ALU = mybir.AluOpType
AF = mybir.ActivationFunctionType

T = 2048
D = 1024
NT = T // 128
ND = D // 128
NH = 16
DH = 64
FF = 4096
NF = FF // 128

ENGINES = ("sync", "tensor", "vector", "scalar", "gpsimd")


class Prog:
    def __init__(self, nc, ctx):
        self.nc = nc
        self.ctx = ctx
        self.octx = ctx
        self.ops = {e: [] for e in ENGINES}
        self.sems = {}
        self.cnt = {}
        self.buf = {}
        self.prefix = ""
        self._uid = 0
        self.bgq = []
        self.bgq_init = None

    def pump(self, n=1):
        for _ in range(n):
            if not getattr(self, "bgq", None):
                return
            self.bgq.pop(0)()

    def flush_bg(self):
        while getattr(self, "bgq", None):
            self.bgq.pop(0)()

    def block_barrier(self):
        snap = [(s, n) for s, n in self.cnt.items() if n > 0]
        for eng in ENGINES:
            self.do(eng, lambda e: None, wait=snap)

    def uid(self):
        self._uid = (self._uid % 8) + 1
        return self._uid

    def dma_totals(self):
        return [(s, n) for s, n in self.cnt.items() if s.startswith("q") and n > 0]

    def sem(self, name):
        if name not in self.sems:
            self.sems[name] = self.octx.enter_context(self.nc.semaphore(name))
            self.cnt[name] = 0
        return self.sems[name]

    def sbuf(self, name, shape, dtype):
        h = self.ctx.enter_context(
            self.nc.sbuf_tensor(self.prefix + name, shape, dtype))
        self.buf[name] = h
        return h

    def psum(self, name, shape, dtype=F32):
        h = self.ctx.enter_context(self.nc.psum_tensor(shape, dtype))
        self.buf[name] = h
        return h

    def do(self, eng, fn, wait=(), inc=None, drain=False):
        if inc is not None:
            if isinstance(inc, str):
                step = 16 if eng in ("sync",) or "dma" in inc or inc.startswith("q") else 1
                inc = (inc, step)
            self.sem(inc[0])
            self.cnt[inc[0]] += inc[1]
            inc_t = (inc[0], self.cnt[inc[0]])
        else:
            inc_t = None
        wait = tuple(wait)
        for s, _ in wait:
            self.sem(s)

        def op(e, fn=fn, wait=wait, inc=inc, drain=drain):
            for s, n in wait:
                e.wait_ge(self.sems[s], n)
            if drain:
                e.drain()
            ins = fn(e)
            if inc is not None:
                ins.then_inc(self.sems[inc[0]], inc[1])

        self.ops[eng].append(op)
        return inc_t[1] if inc_t else None

    def v(self, fn, wait=(), inc="v"):
        return self.do("vector", fn, wait, inc, drain=True)

    def s(self, fn, wait=(), inc="a"):
        return self.do("scalar", fn, wait, inc, drain=True)

    def g(self, fn, wait=(), inc="p"):
        return self.do("gpsimd", fn, wait, inc, drain=True)

    def t(self, fn, wait=(), inc="t"):
        return self.do("tensor", fn, wait, inc, drain=False)

    def ld(self, out, in_, wait=(), q="qs"):
        return self.do("sync", lambda e: e.dma_start(out=out, in_=in_), wait, (q, 16))

    def ldg(self, out, in_, wait=(), q="qg"):
        return self.do("gpsimd", lambda e: e.dma_start(out=out, in_=in_), wait, (q, 16))

    def st(self, out, in_, wait=(), q="qa"):
        return self.do("scalar", lambda e: e.dma_start(out=out, in_=in_), wait, (q, 16))

    def c(self, name):
        return self.cnt.get(name, 0)

    def finalize(self):
        nc = self.nc
        with nc.Block() as block:
            for eng in ENGINES:
                if not self.ops[eng]:
                    continue
                deco = getattr(block, eng)
                ops = self.ops[eng]

                def runner(e, ops=ops):
                    for op in ops:
                        op(e)

                deco(runner)


def emit_load_x(p, x_dram, xr, xb=None):
    for c in range(NT):
        p.ld(xr[:, c * D:(c + 1) * D], x_dram[c * 128:(c + 1) * 128, :])
        if xb is not None:
            p.ldg(xb[:, c * D:(c + 1) * D], x_dram[c * 128:(c + 1) * 128, :])


def emit_transpose(p, src, dst, ident, n_tt=NT, n_ft=ND, src_wait=None,
                   psname="ps_tr"):
    W = n_ft * 128
    Tt = n_tt * 128
    ps = p.buf.get(psname) or p.psum(psname, [128, 512], BF16)
    group = []
    for f in range(n_ft):
        for c in range(n_tt):
            k = len(group)
            w = list(src_wait(c) if callable(src_wait) else (src_wait or ()))
            if k == 0 and p.c("v") > 0:
                w.append(("v", p.c("v")))
            p.t(lambda e, f=f, c=c, k=k, w=w: e.transpose(
                ps[:, k * 128:(k + 1) * 128],
                src[:, c * W + f * 128: c * W + (f + 1) * 128],
                ident[:]), wait=w)
            group.append((f, c))
            if len(group) == 4 or (f == n_ft - 1 and c == n_tt - 1):
                tcnt = p.c("t")
                f0, c0 = group[0]
                n = len(group)
                p.v(lambda e, f0=f0, c0=c0, n=n: e.tensor_copy(
                    out=dst[:, f0 * Tt + c0 * 128: f0 * Tt + (c0 + n) * 128],
                    in_=ps[:, :n * 128]), wait=[("t", tcnt)])
                group = []


def emit_bcast_row(p, row_ap, dst, ncols, ones_row, psname="ps_bc", wait0=()):
    ps = p.buf.get(psname) or p.psum(psname, [128, 512], F32)
    for j in range(0, ncols, 512):
        w = min(512, ncols - j)
        wv = ([("v", p.c("v"))] if p.c("v") else []) + list(wait0)
        tc = p.t(lambda e, j=j, w=w: e.matmul(
            ps[:, :w], ones_row[:, :], row_ap[:, j:j + w],
            start=True, stop=True), wait=wv)
        p.v(lambda e, j=j, w=w: e.tensor_copy(out=dst[:, j:j + w], in_=ps[:, :w]),
            wait=[("t", tc)])



def emit_quant_to_fm(p, src_tiles, n_tiles, width, q32, u, e1, qscr, dst_fm,
                     rscale_col, ident, wait_fn=None, psname="ps_tr"):
    n_ft = width // 128
    Tt = n_tiles * 128
    ps = p.buf.get(psname) or p.psum(psname, [128, 512], BF16)
    for c in range(n_tiles):
        w = list(wait_fn(c)) if wait_fn else []
        if c > 0:
            w.append(("t", p.c("t")))
        p.v(lambda e, c=c, w=w: e.tensor_scalar(
            out=q32[:, :width], in0=src_tiles(c), scalar1=rscale_col(c),
            scalar2=None, op0=ALU.mult), wait=w)
        p.v(lambda e, c=c: e.scalar_tensor_tensor(
            out=u[:, :width], in0=src_tiles(c), scalar=rscale_col(c),
            in1=q32[:, :width], op0=ALU.mult, op1=ALU.subtract))
        p.v(lambda e: e.scalar_tensor_tensor(
            out=e1[:, :width], in0=u[:, :width], scalar=0.5, in1=q32[:, :width],
            op0=ALU.is_ge, op1=ALU.add))
        vq = p.v(lambda e: e.scalar_tensor_tensor(
            out=qscr[:, :width], in0=u[:, :width], scalar=-0.5, in1=e1[:, :width],
            op0=ALU.is_le, op1=ALU.subtract))
        for f0 in range(0, n_ft, 4):
            g = min(4, n_ft - f0)
            wv = [("v", vq), ("a", p.c("a"))]
            for k in range(g):
                p.t(lambda e, f0=f0, k=k: e.transpose(
                    ps[:, k * 128:(k + 1) * 128],
                    qscr[:, (f0 + k) * 128:(f0 + k + 1) * 128],
                    ident[:]), wait=(wv if k == 0 else []))
            tc = p.c("t")
            for k in range(g):
                p.s(lambda e, f0=f0, k=k, c=c: e.activation(
                    out=dst_fm[:, (f0 + k) * Tt + c * 128:(f0 + k) * Tt + (c + 1) * 128],
                    in_=ps[:, k * 128:(k + 1) * 128], func=AF.Copy),
                    wait=[("t", tc)])


def emit_quant_act(p, src_tiles, n_tiles, width, q32, u, e1, dst_tiles,
                   rscale_col, wait_fn=None):
    for c in range(n_tiles):
        w = list(wait_fn(c)) if wait_fn else []
        p.v(lambda e, c=c, w=w: e.tensor_scalar(
            out=q32[:, :width], in0=src_tiles(c), scalar1=rscale_col(c),
            scalar2=None, op0=ALU.mult), wait=w)
        p.v(lambda e, c=c: e.scalar_tensor_tensor(
            out=u[:, :width], in0=src_tiles(c), scalar=rscale_col(c),
            in1=q32[:, :width], op0=ALU.mult, op1=ALU.subtract))
        p.v(lambda e: e.scalar_tensor_tensor(
            out=e1[:, :width], in0=u[:, :width], scalar=0.5, in1=q32[:, :width],
            op0=ALU.is_ge, op1=ALU.add))
        p.v(lambda e, c=c: e.scalar_tensor_tensor(
            out=dst_tiles(c), in0=u[:, :width], scalar=-0.5, in1=e1[:, :width],
            op0=ALU.is_le, op1=ALU.subtract))


def emit_amax_rscale(p, src_tiles, n_tiles, width, amax_buf, rscale_buf,
                     eps, wait_fn=None):
    for c in range(n_tiles):
        w = list(wait_fn(c)) if wait_fn else []
        p.v(lambda e, c=c, w=w: e.reduce_max(
            out=amax_buf[:, c:c + 1], in_=src_tiles(c), axis=AX,
            apply_absolute_value=True), wait=w)
    p.v(lambda e: e.tensor_scalar(
        out=amax_buf[:, :n_tiles], in0=amax_buf[:, :n_tiles], scalar1=float(eps),
        scalar2=None, op0=ALU.max))
    p.v(lambda e: e.reciprocal(out=rscale_buf[:, :n_tiles], in_=amax_buf[:, :n_tiles]))
    p.v(lambda e: e.tensor_scalar(
        out=rscale_buf[:, :n_tiles], in0=rscale_buf[:, :n_tiles], scalar1=127.0,
        scalar2=None, op0=ALU.mult))


def emit_ln_tile(p, c, pre_ap, out_dram, scratch, stats, eps_col, wait=(),
                 out_sbuf_ap=None, out_bf_ap=None, last_store=(0,)):
    msum = stats[:, 0:1]
    ssum = stats[:, 1:2]
    mean = stats[:, 2:3]
    ssd = stats[:, 3:4]
    negvar = stats[:, 4:5]
    rstd = stats[:, 5:6]
    w = list(wait)
    if last_store[0]:
        w.append(("qa", last_store[0]))
    ac = p.s(lambda e: e.activation(
        out=scratch[:, :D], in_=pre_ap, func=AF.Square, accum_out=ssum), wait=w)
    p.v(lambda e: e.reduce_sum(out=msum, in_=pre_ap, axis=AX), wait=w)
    p.v(lambda e: e.tensor_scalar(
        out=mean, in0=msum, scalar1=1.0 / D, scalar2=None, op0=ALU.mult))
    p.v(lambda e: e.tensor_scalar(
        out=ssd, in0=ssum, scalar1=1.0 / D, scalar2=None, op0=ALU.mult),
        wait=[("a", ac)])
    p.v(lambda e: e.scalar_tensor_tensor(
        out=negvar, in0=mean, scalar=mean, in1=ssd, op0=ALU.mult,
        op1=ALU.subtract))
    ac2 = p.s(lambda e: e.activation(
        out=rstd, in_=negvar, func=AF.Sqrt, bias=eps_col, scale=-1.0),
        wait=[("v", p.c("v"))])
    p.v(lambda e: e.reciprocal(out=rstd, in_=rstd), wait=[("a", ac2)])
    vq = p.v(lambda e: e.tensor_scalar(
        out=scratch[:, :D], in0=pre_ap, scalar1=mean, scalar2=rstd,
        op0=ALU.subtract, op1=ALU.mult))
    if out_dram is not None:
        last_store[0] = p.st(out_dram[c * 128:(c + 1) * 128, :],
                             scratch[:, :D], wait=[("v", vq)])
    if out_sbuf_ap is not None:
        p.v(lambda e: e.tensor_copy(out=out_sbuf_ap, in_=scratch[:, :D]))
    if out_bf_ap is not None:
        p.s(lambda e: e.activation(out=out_bf_ap, in_=scratch[:, :D],
                                   func=AF.Copy), wait=[("v", vq)])


def emit_ln_store2(p, pre_tiles, out_dram, scratch, stats, eps_col,
                   wait_fn=None, out_sbuf=None, out_bf=None):
    last_store = [0]
    for c in range(NT):
        w = list(wait_fn(c)) if wait_fn else []
        emit_ln_tile(p, c, pre_tiles(c), out_dram, scratch, stats, eps_col,
                     wait=w,
                     out_sbuf_ap=(out_sbuf[:, c * D:(c + 1) * D] if out_sbuf is not None else None),
                     out_bf_ap=(out_bf[:, c * D:(c + 1) * D] if out_bf is not None else None),
                     last_store=last_store)


def emit_ternary(p, src_tiles, dst_tiles, n_tiles, width, rs_col, d1, d2,
                 wait_fn=None, eng="v", scale_after=None):
    em = p.v if eng == "v" else p.g
    for i in range(n_tiles):
        w = list(wait_fn(i)) if wait_fn else []
        em(lambda e, i=i, w=w: e.tensor_scalar(
            out=d1[:, :width], in0=src_tiles(i), scalar1=rs_col, scalar2=0.5,
            op0=ALU.mult, op1=ALU.is_ge), wait=w)
        em(lambda e, i=i: e.tensor_scalar(
            out=d2[:, :width], in0=src_tiles(i), scalar1=rs_col, scalar2=-0.5,
            op0=ALU.mult, op1=ALU.is_le))
        if scale_after is None:
            em(lambda e, i=i: e.tensor_sub(
                out=dst_tiles(i), in0=d1[:, :width], in1=d2[:, :width]))
        else:
            em(lambda e, i=i: e.tensor_sub(
                out=d1[:, :width], in0=d1[:, :width], in1=d2[:, :width]))
            em(lambda e, i=i: e.tensor_scalar(
                out=dst_tiles(i), in0=d1[:, :width], scalar1=scale_after,
                scalar2=None, op0=ALU.mult))


def emit_global_absmean(p, src_tiles, n_tiles, width, partial, ones_col,
                        out11, nelem, eps, wait_fn=None):
    ps = p.buf.get("ps_sc") or p.psum("ps_sc", [1, 16], F32)
    for i in range(n_tiles):
        w = list(wait_fn(i)) if wait_fn else []
        p.v(lambda e, i=i, w=w: e.reduce_sum(
            out=partial[:, 1:2], in_=src_tiles(i), axis=AX,
            apply_absolute_value=True), wait=w)
        if i == 0:
            p.v(lambda e: e.tensor_copy(out=partial[:, 0:1], in_=partial[:, 1:2]))
        else:
            p.v(lambda e: e.tensor_add(
                out=partial[:, 0:1], in0=partial[:, 0:1], in1=partial[:, 1:2]))
    vc = p.c("v")
    tc = p.t(lambda e: e.matmul(ps[:, 0:1], partial[:, 0:1], ones_col[:],
                                start=True, stop=True), wait=[("v", vc)])
    p.v(lambda e: e.tensor_scalar(
        out=out11, in0=ps[0:1, 0:1], scalar1=1.0 / nelem, scalar2=float(eps),
        op0=ALU.mult, op1=ALU.max), wait=[("t", tc)])


def emit_col_bcast_scalar(p, src11, dst_col, ones_row128):
    ps = p.buf.get("ps_sc") or p.psum("ps_sc", [1, 16], F32)
    ps2 = p.buf.get("ps_sc2") or p.psum("ps_sc2", [128, 16], F32)
    vc = p.c("v")
    tc = p.t(lambda e: e.matmul(ps2[:, 0:1], ones_row128[:, :], src11,
                                start=True, stop=True), wait=[("v", vc)])
    p.v(lambda e: e.tensor_copy(out=dst_col, in_=ps2[:, 0:1]), wait=[("t", tc)])


def _declare_l2prep(dtfp):
    return {
        "wgt": dtfp("wgt", [D, FF], F32).ap(),
        "wut": dtfp("wut", [D, FF], F32).ap(),
        "wdt": dtfp("wdt", [FF, D], F32).ap(),
        "wgscr": dtfp("wgscr", [8 * 128, ND * 512], BF16, kind="Internal").ap(),
        "wuscr": dtfp("wuscr", [8 * 128, ND * 512], BF16, kind="Internal").ap(),
        "wdscr": dtfp("wdscr", [FF, D], BF16, kind="Internal").ap(),
        "rs_d": dtfp("rs_d", [128, 8], F32, kind="Internal").ap(),
        "mm_d": dtfp("mm_d", [1, 8], F32, kind="Internal").ap(),
    }


def _declare_l0prep(dtfp):
    return {
        "w1t": dtfp("w1t", [D, FF], F32).ap(),
        "w2t": dtfp("w2t", [FF, D], F32).ap(),
        "w1scr": dtfp("w1scr", [NF * 128, 1024], BF16, kind="Internal").ap(),
        "w2scr": dtfp("w2scr", [FF, 1024], BF16, kind="Internal").ap(),
        "rs_d": dtfp("rs_d", [128, 8], F32, kind="Internal").ap(),
        "mm_d": dtfp("mm_d", [1, 8], F32, kind="Internal").ap(),
    }


def _prep_closures(p, specs, rs_d, mm_d, n_rs_cols):
    stage = p.sbuf("pp_stage", [128, 2 * 1024], F32)
    d1 = p.sbuf("pp_d1", [128, 1024], BF16)
    d2 = p.sbuf("pp_d2", [128, 1024], BF16)
    qscr = p.sbuf("pp_q", [128, 1024], BF16)
    acc = p.sbuf("pp_acc", [128, 2], F32)
    m11 = p.sbuf("pp_m11", [1, 8], F32)
    rs1p = p.sbuf("pp_rs1", [128, 8], F32)
    onescp = p.sbuf("pp_onesc", [128, 1], F32)
    onesr = p.buf["onesr"]
    p.v(lambda e: e.memset(onescp[:], 1.0))
    p.v(lambda e: e.memset(rs1p[:, :], 0.0))
    p.v(lambda e: e.memset(m11[:, :], 0.0))
    cl = []
    for sp in specs:
        av, a_nt, a_nc = sp["av"], sp["a_nt"], sp["a_nc"]
        for i in range(a_nt):
            def c(i=i, av=av, a_nc=a_nc):
                sl = stage[:, (i % 2) * 1024:(i % 2) * 1024 + a_nc]
                qn = "qpa" if i % 2 == 0 else "qpb"
                w = [("v", p.c("v"))] + (list(p.dma_totals()) if i < 2 else [])
                q = p.ld(sl, av[i * 128:(i + 1) * 128, :], wait=w, q=qn)
                p.v(lambda e, sl=sl: e.reduce_sum(
                    out=acc[:, 1:2], in_=sl, axis=AX,
                    apply_absolute_value=True), wait=[(qn, q)])
                if i == 0:
                    p.v(lambda e: e.tensor_copy(out=acc[:, 0:1], in_=acc[:, 1:2]))
                else:
                    p.v(lambda e: e.tensor_add(out=acc[:, 0:1], in0=acc[:, 0:1],
                                               in1=acc[:, 1:2]))
            cl.append(c)

        def fin(sp=sp):
            j = sp["mcol"]
            ps = p.buf["ps_sc"]
            vc = p.c("v")
            tc = p.t(lambda e, ps=ps: e.matmul(
                ps[0:1, 0:1], acc[:, 0:1], onescp[:],
                start=True, stop=True), wait=[("v", vc)])
            p.v(lambda e, ps=ps, sp=sp, j=j: e.tensor_scalar(
                out=m11[0:1, 2 * j:2 * j + 1], in0=ps[0:1, 0:1],
                scalar1=1.0 / (sp["a_nt"] * 128 * sp["a_nc"]), scalar2=1e-5,
                op0=ALU.mult, op1=ALU.max), wait=[("t", tc)])
            p.v(lambda e, j=j: e.reciprocal(
                out=m11[0:1, 2 * j + 1:2 * j + 2],
                in_=m11[0:1, 2 * j:2 * j + 1]))
            emit_col_bcast_scalar(p, m11[0:1, 2 * j + 1:2 * j + 2],
                                  rs1p[:, sp["rcol"]:sp["rcol"] + 1], onesr)
            if sp.get("meancol") is not None:
                emit_col_bcast_scalar(p, m11[0:1, 2 * j:2 * j + 1],
                                      rs1p[:, sp["meancol"]:sp["meancol"] + 1],
                                      onesr)
        cl.append(fin)

        for i in range(sp["t_nt"]):
            def c2(i=i, sp=sp):
                t_nc = sp["t_nc"]
                sl = stage[:, (i % 2) * 1024:(i % 2) * 1024 + t_nc]
                qn = "qpa" if i % 2 == 0 else "qpb"
                w = [("v", p.c("v")), ("qps", p.c("qps"))]
                if i < 2:
                    w += list(p.dma_totals())
                q = p.ld(sl, sp["tv"](i), wait=w, q=qn)
                rc = sp["rcol"]
                p.v(lambda e, sl=sl, t_nc=t_nc, rc=rc: e.tensor_scalar(
                    out=d1[:, :t_nc], in0=sl, scalar1=rs1p[:, rc:rc + 1],
                    scalar2=0.5, op0=ALU.mult, op1=ALU.is_ge), wait=[(qn, q)])
                p.v(lambda e, sl=sl, t_nc=t_nc, rc=rc: e.tensor_scalar(
                    out=d2[:, :t_nc], in0=sl, scalar1=rs1p[:, rc:rc + 1],
                    scalar2=-0.5, op0=ALU.mult, op1=ALU.is_le))
                vt = p.v(lambda e, t_nc=t_nc: e.tensor_sub(
                    out=qscr[:, :t_nc], in0=d1[:, :t_nc], in1=d2[:, :t_nc]))
                for (dst, c0, c1) in sp["stores"](i):
                    p.do("gpsimd",
                         lambda e, dst=dst, c0=c0, c1=c1: e.dma_start(
                             out=dst, in_=qscr[:, c0:c1]),
                         wait=[("v", vt)], inc=("qps", 16))
            cl.append(c2)

    def fin_store():
        vv = p.c("v")
        p.do("gpsimd", lambda e: e.dma_start(out=rs_d, in_=rs1p[:, :]),
             wait=[("v", vv)], inc=("qps", 16))
        p.do("gpsimd", lambda e: e.dma_start(out=mm_d, in_=m11[0:1, :]),
             inc=("qps", 16))
    cl.append(fin_store)
    return cl


def _init_l2prep(p, H):
    p.buf["ps_sc"] = p.buf["ps_y"]
    p.buf["ps_sc2"] = p.buf["ps_y2"]
    wgv = H["wgt"].rearrange("a (b c) -> (a b) c", b=4)
    wuv = H["wut"].rearrange("a (b c) -> (a b) c", b=4)
    specs = []
    for j, (wd_, wvw, scr) in enumerate(((H["wgt"], wgv, H["wgscr"]),
                                         (H["wut"], wuv, H["wuscr"]))):
        def stores(i, scr=scr):
            k, hh = i // 4, i % 4
            return [(scr[(hh * 2 + bb) * 128:(hh * 2 + bb + 1) * 128,
                         k * 512:(k + 1) * 512], bb * 512, (bb + 1) * 512)
                    for bb in range(2)]
        specs.append(dict(
            av=wvw, a_nt=32, a_nc=1024,
            tv=(lambda i, wd_=wd_: wd_[(i // 4) * 128:(i // 4 + 1) * 128,
                                       (i % 4) * 1024:(i % 4 + 1) * 1024]),
            t_nt=32, t_nc=1024, stores=stores,
            mcol=j, rcol=j, meancol=3 + j))
    specs.append(dict(
        av=H["wdt"], a_nt=32, a_nc=1024,
        tv=(lambda i: H["wdt"][i * 128:(i + 1) * 128, :]),
        t_nt=32, t_nc=1024,
        stores=(lambda i: [(H["wdscr"][i * 128:(i + 1) * 128, :], 0, 1024)]),
        mcol=2, rcol=2, meancol=5))
    return _prep_closures(p, specs, H["rs_d"], H["mm_d"], 6)


def _init_l0prep(p, H):
    p.buf["ps_sc"] = p.buf["ps_y"]
    p.buf["ps_sc2"] = p.buf["ps_y2"]
    w1v = H["w1t"].rearrange("a (b c) -> (a b) c", b=4)
    def stores1(i):
        k, hh = i // 4, i % 4
        return [(H["w1scr"][(hh * 8 + ii) * 128:(hh * 8 + ii + 1) * 128,
                            k * 128:(k + 1) * 128], ii * 128, (ii + 1) * 128)
                for ii in range(8)]
    specs = [
        dict(av=w1v, a_nt=32, a_nc=1024,
             tv=(lambda i: H["w1t"][(i // 4) * 128:(i // 4 + 1) * 128,
                                    (i % 4) * 1024:(i % 4 + 1) * 1024]),
             t_nt=32, t_nc=1024, stores=stores1,
             mcol=0, rcol=0, meancol=None),
        dict(av=H["w2t"], a_nt=32, a_nc=1024,
             tv=(lambda i: H["w2t"][i * 128:(i + 1) * 128, :]),
             t_nt=32, t_nc=1024,
             stores=(lambda i: [(H["w2scr"][i * 128:(i + 1) * 128, :], 0, 1024)]),
             mcol=1, rcol=1, meancol=2),
    ]
    return _prep_closures(p, specs, H["rs_d"], H["mm_d"], 3)


def common_setup(p, dtf, x_in, x_out, need_xb=True):
    ident = dtf("ident", [128, 128], BF16, kind="ExternalInput").ap()
    ones_row = dtf("ones_row", [1, 128], F32, kind="ExternalInput").ap()
    xr = p.sbuf("xr", [128, NT * D], F32)
    identb = p.sbuf("identb", [128, 128], BF16)
    onesr = p.sbuf("onesr", [1, 128], F32)
    p.ld(identb[:], ident)
    p.ld(onesr[:], ones_row)
    cst = p.sbuf("cst", [128, 4], F32)
    p.v(lambda e: e.memset(cst[:, 0:1], 1e-5))
    xb = None
    if need_xb:
        xb = p.sbuf("xb", [128, NT * D], BF16)
    emit_load_x(p, x_in, xr, xb)
    return xr, xb, identb, onesr, cst


def emit_l1ffn(p, dtf, x_in, x_out):
    if True:
        nc = p.nc
        ctx = p.ctx
        xr, xb, identb, onesr, cst = common_setup(p, dtf, x_in, x_out, need_xb=False)
        w_d = dtf("w", [D, D], F32, kind="ExternalInput").ap()
        b_d = dtf("b", [1, D], F32, kind="ExternalInput").ap()

        wt = p.sbuf("wt", [128, ND * D], F32)
        for i in range(ND):
            p.ld(wt[:, i * D:(i + 1) * D], w_d[i * 128:(i + 1) * 128, :])
        brow = p.sbuf("brow", [1, D], F32)
        p.ld(brow[:], b_d)
        ld_done = p.c("qs")

        q32 = p.sbuf("q32", [128, D], I32)
        u = p.sbuf("u", [128, D], F32)
        e1b = p.sbuf("e1b", [128, D], F32)
        stats = p.sbuf("stats", [128, 16], F32)
        amax = p.sbuf("amax", [128, NT], F32)
        rsc = p.sbuf("rsc", [128, NT], F32)
        sxn = p.sbuf("sxn", [128, NT], F32)
        qscr = p.sbuf("qscr", [128, D], BF16)
        xq_fm = p.sbuf("xq_fm", [128, ND * T], BF16)
        wq = p.sbuf("wq", [128, ND * D], BF16)
        wqT = p.sbuf("wqT", [128, ND * D], BF16)
        wamax = p.sbuf("wamax", [128, ND], F32)
        wrsc = p.sbuf("wrsc", [128, ND], F32)
        wsx = p.sbuf("wsx", [128, ND], F32)
        biasT = p.sbuf("biasT", [128, D], F32)
        y1 = p.sbuf("y1", [128, D], F32)
        scr = p.sbuf("scr", [128, D], F32)

        ps_mm = p.psum("ps_mm", [128, 512], F32)
        ps_mm2 = p.psum("ps_mm2", [128, 512], F32)

        emit_bcast_row(p, brow, biasT, D, onesr, wait0=[("qs", ld_done)])

        emit_amax_rscale(p, lambda c: xr[:, c * D:(c + 1) * D], NT, D,
                         amax, rsc, 1e-8)
        p.v(lambda e: e.tensor_scalar(out=sxn[:, :NT], in0=amax[:, :NT],
                                      scalar1=-1.0 / 127.0, scalar2=None,
                                      op0=ALU.mult))
        emit_quant_to_fm(p, lambda c: xr[:, c * D:(c + 1) * D], NT, D, q32, u,
                         e1b, qscr, xq_fm, lambda c: rsc[:, c:c + 1], identb)

        for i in range(ND):
            p.v(lambda e, i=i: e.reduce_max(
                out=wamax[:, i:i + 1], in_=wt[:, i * D:(i + 1) * D], axis=AX,
                apply_absolute_value=True))
        p.v(lambda e: e.tensor_scalar(out=wamax[:, :ND], in0=wamax[:, :ND],
                                      scalar1=1e-8, scalar2=None, op0=ALU.max))
        p.v(lambda e: e.reciprocal(out=wrsc[:, :ND], in_=wamax[:, :ND]))
        p.v(lambda e: e.tensor_scalar(out=wrsc[:, :ND], in0=wrsc[:, :ND],
                                      scalar1=127.0, scalar2=None, op0=ALU.mult))
        p.v(lambda e: e.tensor_scalar(out=wsx[:, :ND], in0=wamax[:, :ND],
                                      scalar1=-1.0 / 127.0, scalar2=None,
                                      op0=ALU.mult))
        for i in range(ND):
            p.v(lambda e, i=i: e.tensor_scalar(
                out=q32[:, :D], in0=wt[:, i * D:(i + 1) * D],
                scalar1=wrsc[:, i:i + 1], scalar2=None, op0=ALU.mult))
            p.v(lambda e, i=i: e.scalar_tensor_tensor(
                out=u[:, :D], in0=wt[:, i * D:(i + 1) * D],
                scalar=wrsc[:, i:i + 1], in1=q32[:, :D], op0=ALU.mult,
                op1=ALU.subtract))
            p.v(lambda e: e.scalar_tensor_tensor(
                out=e1b[:, :D], in0=u[:, :D], scalar=0.5, in1=q32[:, :D],
                op0=ALU.is_ge, op1=ALU.add))
            p.v(lambda e: e.scalar_tensor_tensor(
                out=u[:, :D], in0=u[:, :D], scalar=-0.5, in1=e1b[:, :D],
                op0=ALU.is_le, op1=ALU.subtract))
            p.v(lambda e, i=i: e.tensor_scalar(
                out=wq[:, i * D:(i + 1) * D], in0=u[:, :D],
                scalar1=wsx[:, i:i + 1], scalar2=None, op0=ALU.mult))
        vw_done = p.c("v")
        emit_transpose(p, wq, wqT, identb, ND, ND,
                       src_wait=lambda c: [("v", vw_done)], psname="ps_tr2")

        fm_done_v = p.c("v")
        fm_done_a = p.c("a")
        last_evac = {0: 0, 1: 0}
        for c in range(NT):
            for nn in range(2):
                ps = ps_mm if nn == 0 else ps_mm2
                for f in range(ND):
                    w = []
                    if f == 0:
                        if c == 0:
                            w = [("v", fm_done_v), ("a", fm_done_a)]
                        elif last_evac[nn]:
                            w = [("v", last_evac[nn])]
                    p.t(lambda e, c=c, nn=nn, f=f, ps=ps: e.matmul(
                        ps[:, :512],
                        xq_fm[:, f * T + c * 128: f * T + (c + 1) * 128],
                        wqT[:, f * D + nn * 512: f * D + (nn + 1) * 512],
                        start=(f == 0), stop=(f == ND - 1)), wait=w)
                tc = p.c("t")
                last_evac[nn] = p.v(lambda e, c=c, nn=nn, ps=ps: e.scalar_tensor_tensor(
                    out=y1[:, nn * 512:(nn + 1) * 512], in0=ps[:, :512],
                    scalar=sxn[:, c:c + 1], in1=biasT[:, nn * 512:(nn + 1) * 512],
                    op0=ALU.mult, op1=ALU.add), wait=[("t", tc)])
                p.v(lambda e, c=c, nn=nn: e.tensor_add(
                    out=xr[:, c * D + nn * 512: c * D + (nn + 1) * 512],
                    in0=y1[:, nn * 512:(nn + 1) * 512],
                    in1=xr[:, c * D + nn * 512: c * D + (nn + 1) * 512]))

        pre_done = p.c("v")
        emit_ln_store2(p, lambda c: xr[:, c * D:(c + 1) * D], x_out, scr,
                       stats, cst[:, 0:1], wait_fn=lambda c: [("v", pre_done)])


_IDENT_BF = np.eye(128, dtype=ml_dtypes.bfloat16)
_ONES_ROW = np.ones((1, 128), dtype=np.float32)
_ONES_BF = np.ones((1, 128), dtype=ml_dtypes.bfloat16)
_ONES_CB = np.ones((128, 1), dtype=ml_dtypes.bfloat16)


def _causal_masks():
    m = np.zeros((4 * 128, 512), dtype=ml_dtypes.bfloat16)
    r = np.arange(128)[:, None]
    q = np.arange(512)[None, :]
    for di in range(4):
        m[di * 128:(di + 1) * 128, :] = (r <= q - 128 * di).astype(ml_dtypes.bfloat16)
    return m


_MASKS = _causal_masks()


def _causal_bias_T():
    m = np.zeros((4 * 128, 512), dtype=np.float32)
    r = np.arange(128)[:, None]
    q = np.arange(512)[None, :]
    for di in range(4):
        m[di * 128:(di + 1) * 128, :] = np.where(r <= q - 128 * di, 0.0, -8192.0)
    return m.astype(ml_dtypes.bfloat16)


_MASKS_BIAS_T = _causal_bias_T()


def _rope_tables():
    dh = 64
    inv_freq = 1.0 / (10000.0 ** (np.arange(0, dh, 2, dtype=np.float64) / dh))
    pos = np.arange(T, dtype=np.float64)
    ang = pos[None, :] * inv_freq[:, None]
    C = np.zeros((128, T), dtype=np.float64)
    S = np.zeros((128, T), dtype=np.float64)
    for b in (0, 64):
        C[b:b + 32] = np.cos(ang)
        C[b + 32:b + 64] = np.cos(ang)
        S[b:b + 32] = np.sin(ang)
        S[b + 32:b + 64] = -np.sin(ang)
    return C.astype(np.float32), S.astype(np.float32)


_ROPE_C, _ROPE_S = _rope_tables()


def _l3_tables():
    K = 32
    n = np.arange(1, K + 1, dtype=np.float64)
    t = 10.0 + 6.0 * n
    for _ in range(60):
        f = t / (2 * np.pi) * np.log(t / (2 * np.pi * np.e)) - (n - 0.875)
        fp = np.log(t / (2 * np.pi)) / (2 * np.pi)
        t = t - f / fp
    g = t / t[0]
    denom = 0.25 + g * g
    w_re = (0.5 / denom)
    w_im = (-g / denom)
    pos = np.arange(T, dtype=np.float64)
    inv_freq = 10000.0 ** (-np.arange(K, dtype=np.float64) / K)
    phase = pos[:, None] * inv_freq[None, :]
    cosp = np.cos(phase)
    sinp = np.sin(phase)
    A = w_re[None, :] * cosp + w_im[None, :] * sinp
    Bp = w_re[None, :] * sinp - w_im[None, :] * cosp
    bdec = 1.0 / np.sqrt(1.0 + pos)
    adec = np.sqrt(1.0 + pos)
    Aq = np.zeros((T, D), np.float64)
    Bq = np.zeros((T, D), np.float64)
    Ck = np.zeros((T, D), np.float64)
    Sk = np.zeros((T, D), np.float64)
    for h in range(NH):
        b = h * 64
        Aq[:, b + 0:b + 64:2] = A * bdec[:, None]
        Aq[:, b + 1:b + 64:2] = A * bdec[:, None]
        Bq[:, b + 0:b + 64:2] = Bp * bdec[:, None]
        Bq[:, b + 1:b + 64:2] = -Bp * bdec[:, None]
        Ck[:, b + 0:b + 64:2] = cosp * adec[:, None]
        Ck[:, b + 1:b + 64:2] = cosp * adec[:, None]
        Sk[:, b + 0:b + 64:2] = sinp * adec[:, None]
        Sk[:, b + 1:b + 64:2] = -sinp * adec[:, None]
    return (Aq.astype(np.float32), Bq.astype(np.float32),
            Ck.astype(np.float32), Sk.astype(np.float32))


_L3T = None


def _l3_tables_cached():
    global _L3T
    if _L3T is None:
        _L3T = _l3_tables()
    return _L3T


def _causal_masks_S():
    m = np.zeros((4 * 128, 512), dtype=ml_dtypes.bfloat16)
    r = np.arange(128)[:, None]
    q = np.arange(512)[None, :]
    for di in range(4):
        m[di * 128:(di + 1) * 128, :] = (q <= 128 * di + r).astype(ml_dtypes.bfloat16)
    return m


_MASKS_S = _causal_masks_S()


def _trimask_neg():
    r = np.arange(128)[:, None]
    c = np.arange(128)[None, :]
    return np.where(c <= r, 0.0, -8192.0).astype(ml_dtypes.bfloat16)


_TRIMASK_NEG = _trimask_neg()


def _shard_x(x):
    return [np.ascontiguousarray(x[i]) for i in range(8)]




def emit_stream_x_fm(p, x_dram, x_fm, stage, ident, psname="ps_tr", wait0=()):
    ps = p.buf.get(psname) or p.psum(psname, [128, 512], BF16)
    u = p.uid()
    for c in range(NT):
        sl = stage[:, (c % 2) * D:((c % 2) + 1) * D]
        qn = "qx%d_%d" % (c % 2, u)
        w = []
        if c >= 2:
            w.append(("t", p.c("t")))
        qg = p.ldg(sl, x_dram[c * 128:(c + 1) * 128, :], wait=w, q=qn)
        for f0 in range(0, ND, 4):
            wv = [(qn, qg), ("a", p.c("a"))] + (list(wait0) if c == 0 else [])
            for k in range(4):
                p.t(lambda e, f0=f0, k=k, sl=sl: e.transpose(
                    ps[:, k * 128:(k + 1) * 128],
                    sl[:, (f0 + k) * 128:(f0 + k + 1) * 128],
                    ident[:]), wait=(wv if k == 0 else []))
            tc = p.c("t")
            for k in range(4):
                p.s(lambda e, f0=f0, k=k, c=c: e.activation(
                    out=x_fm[:, (f0 + k) * T + c * 128:(f0 + k) * T + (c + 1) * 128],
                    in_=ps[:, k * 128:(k + 1) * 128], func=AF.Copy),
                    wait=[("t", tc)])


def emit_absmean_stream(p, w_dram, nrows, ncols, stage, acc, ones_col, out11,
                        eps):
    ntile = nrows // 128
    ps = p.buf.get("ps_sc") or p.psum("ps_sc", [1, 16], F32)
    u = p.uid()
    prior = p.dma_totals()
    for i in range(ntile):
        sl = stage[:, (i % 2) * ncols:((i % 2) + 1) * ncols]
        qn = "qw%d_%d" % (i % 2, u)
        w = [("v", p.c("v"))] + ([] if i >= 2 else list(prior))
        q = p.ld(sl, w_dram[i * 128:(i + 1) * 128, :], wait=w, q=qn)
        p.v(lambda e, sl=sl: e.reduce_sum(
            out=acc[:, 1:2], in_=sl, axis=AX, apply_absolute_value=True),
            wait=[(qn, q)])
        if i == 0:
            p.v(lambda e: e.tensor_copy(out=acc[:, 0:1], in_=acc[:, 1:2]))
        else:
            p.v(lambda e: e.tensor_add(out=acc[:, 0:1], in0=acc[:, 0:1],
                                       in1=acc[:, 1:2]))
    vc = p.c("v")
    tc = p.t(lambda e: e.matmul(ps[0:1, 0:1], acc[:, 0:1], ones_col[:],
                                start=True, stop=True), wait=[("v", vc)])
    p.v(lambda e: e.tensor_scalar(
        out=out11, in0=ps[0:1, 0:1], scalar1=1.0 / (nrows * ncols),
        scalar2=float(eps), op0=ALU.mult, op1=ALU.max), wait=[("t", tc)])


def emit_stream_ternary(p, w_dram, nrows, ncols, stage, rs_col, d1, d2, dst,
                        dst_stride):
    ntile = nrows // 128
    u = p.uid()
    prior = p.dma_totals()
    for i in range(ntile):
        sl = stage[:, (i % 2) * ncols:((i % 2) + 1) * ncols]
        qn = "qw%d_%d" % (i % 2, u)
        w = [("v", p.c("v"))] + ([] if i >= 2 else list(prior))
        q = p.ld(sl, w_dram[i * 128:(i + 1) * 128, :], wait=w, q=qn)
        p.v(lambda e, sl=sl: e.tensor_scalar(
            out=d1[:, :ncols], in0=sl, scalar1=rs_col, scalar2=0.5,
            op0=ALU.mult, op1=ALU.is_ge), wait=[(qn, q)])
        p.v(lambda e, sl=sl: e.tensor_scalar(
            out=d2[:, :ncols], in0=sl, scalar1=rs_col, scalar2=-0.5,
            op0=ALU.mult, op1=ALU.is_le))
        p.v(lambda e, i=i: e.tensor_sub(
            out=dst[:, i * dst_stride:i * dst_stride + ncols],
            in0=d1[:, :ncols], in1=d2[:, :ncols]))


def emit_l3ffn(p, dtf, x_in, x_out):
    if True:
        nc = p.nc
        ctx = p.ctx
        ident = dtf("ident", [128, 128], BF16, kind="ExternalInput").ap()
        onesb_d = dtf("ones_bf", [1, 128], BF16, kind="ExternalInput").ap()
        ones_row = dtf("ones_row", [1, 128], F32, kind="ExternalInput").ap()
        w1t_d = dtf("w1t", [D, FF], F32, kind="ExternalInput").ap()
        b1f_d = dtf("b1f", [128, NF], F32, kind="ExternalInput").ap()
        w2t_d = dtf("w2t", [FF, D], F32, kind="ExternalInput").ap()
        b2r_d = dtf("b2r", [1, D], F32, kind="ExternalInput").ap()
        w1scr = dtf("w1scr", [NF * 128, 1024], BF16, kind="Internal").ap()

        identb = p.sbuf("identb", [128, 128], BF16)
        onesr = p.sbuf("onesr", [1, 128], F32)
        onesb = p.sbuf("onesb", [1, 128], BF16)
        onesc = p.sbuf("onesc", [128, 1], F32)
        b1f = p.sbuf("b1f_s", [128, NF], F32)
        b2r = p.sbuf("b2r_s", [1, D], F32)
        b2b = p.sbuf("b2b_s", [1, D], BF16)
        cst = p.sbuf("cst", [128, 4], F32)
        p.ld(identb[:], ident)
        p.ld(onesr[:], ones_row)
        p.ld(onesb[:], onesb_d)
        p.ld(b1f[:], b1f_d)
        p.ld(b2r[:], b2r_d)
        c0 = p.c("qs")
        p.v(lambda e: e.memset(cst[:, 0:1], 1e-5), wait=[("qs", c0)])
        p.v(lambda e: e.memset(onesc[:], 1.0))
        p.v(lambda e: e.tensor_copy(out=b2b[:], in_=b2r[:]))

        x_fm = p.sbuf("x_fm", [128, ND * T], BF16)
        w2tern = p.sbuf("w2tern", [128, NF * D], BF16)
        stage = p.sbuf("stage", [128, 2 * 2048], F32)
        xstage = p.sbuf("xstage", [128, 2 * D], BF16)
        ternh = p.sbuf("ternh", [128, 2048], BF16)
        hbuf = p.sbuf("hbuf", [128, NF * 512], BF16)
        w1sl = p.sbuf("w1sl", [128, 2 * 1024], BF16)
        xres = p.sbuf("xres", [128, 2 * D], F32)
        pre = p.sbuf("pre", [128, D], F32)
        scr = p.sbuf("scr", [128, D], F32)
        stats = p.sbuf("stats", [128, 16], F32)
        acc = p.sbuf("acc", [128, 2], F32)
        m11 = p.sbuf("m11", [1, 2], F32)
        rs1 = p.sbuf("rs1", [128, 2], F32)
        d1 = p.sbuf("d1", [128, 2048], F32)
        d2 = p.sbuf("d2", [128, 2048], F32)

        ps_h = p.psum("ps_h", [128, 512], F32)
        ps_h2 = p.psum("ps_h2", [128, 512], F32)
        ps_y = p.psum("ps_y", [128, 512], F32)
        ps_y2 = p.psum("ps_y2", [128, 512], F32)

        emit_stream_x_fm(p, x_in, x_fm, xstage, identb,
                         wait0=[("qs", p.c("qs"))])

        w1v = w1t_d.rearrange("a (b c) -> (a b) c", b=2)
        w2v = w2t_d.rearrange("a (b c) -> (a b) c", b=2)
        emit_absmean_stream(p, w1v, 2 * D, 2048, stage, acc, onesc,
                            m11[0:1, 0:1], 1e-8)
        p.v(lambda e: e.reciprocal(out=m11[0:1, 1:2], in_=m11[0:1, 0:1]))
        emit_col_bcast_scalar(p, m11[0:1, 1:2], rs1[:, 0:1], onesr)
        pass
        uq = p.uid()
        prior = p.dma_totals()
        for k in range(ND):
            for hh in range(2):
                sl = stage[:, (hh % 2) * 2048:((hh % 2) + 1) * 2048]
                qn = "qw%d_%d" % (hh % 2, uq)
                w = [("qg", p.c("qg"))] + (list(prior) if k == 0 else [])
                if p.c("v"):
                    w.append(("v", p.c("v")))
                q = p.ld(sl, w1t_d[k * 128:(k + 1) * 128, hh * 2048:(hh + 1) * 2048],
                         wait=w, q=qn)
                p.v(lambda e, sl=sl: e.tensor_scalar(
                    out=d1[:], in0=sl, scalar1=rs1[:, 0:1], scalar2=0.5,
                    op0=ALU.mult, op1=ALU.is_ge), wait=[(qn, q)])
                p.v(lambda e, sl=sl: e.tensor_scalar(
                    out=d2[:], in0=sl, scalar1=rs1[:, 0:1], scalar2=-0.5,
                    op0=ALU.mult, op1=ALU.is_le))
                vt = p.v(lambda e: e.tensor_sub(out=ternh[:], in0=d1[:], in1=d2[:]))
                for ii in range(16):
                    i = hh * 16 + ii
                    p.ldg(w1scr[i * 128:(i + 1) * 128, k * 128:(k + 1) * 128],
                          ternh[:, ii * 128:(ii + 1) * 128], wait=[("v", vt)])

        p.buf["ps_sc"] = ps_y
        p.buf["ps_sc2"] = ps_y2
        w2cl = []
        for idx in range(64):
            def c_am(i=idx):
                sl = stage[:, (i % 2) * 2048:(i % 2) * 2048 + 512]
                qn = "qpa" if i % 2 == 0 else "qpb"
                w = [("v", p.c("v"))] + (list(p.dma_totals()) if i < 2 else [])
                q = p.ld(sl, w2v[i * 128:(i + 1) * 128, :], wait=w, q=qn)
                p.v(lambda e, sl=sl: e.reduce_sum(
                    out=acc[:, 1:2], in_=sl, axis=AX,
                    apply_absolute_value=True), wait=[(qn, q)])
                if i == 0:
                    p.v(lambda e: e.tensor_copy(out=acc[:, 0:1], in_=acc[:, 1:2]))
                else:
                    p.v(lambda e: e.tensor_add(out=acc[:, 0:1], in0=acc[:, 0:1],
                                               in1=acc[:, 1:2]))
            w2cl.append(c_am)

        def c_fin():
            vc = p.c("v")
            tc = p.t(lambda e: e.matmul(ps_y[0:1, 0:1], acc[:, 0:1], onesc[:],
                                        start=True, stop=True), wait=[("v", vc)])
            p.v(lambda e: e.tensor_scalar(
                out=m11[0:1, 0:1], in0=ps_y[0:1, 0:1],
                scalar1=1.0 / (2 * FF * 512), scalar2=1e-8,
                op0=ALU.mult, op1=ALU.max), wait=[("t", tc)])
            p.v(lambda e: e.reciprocal(out=m11[0:1, 1:2], in_=m11[0:1, 0:1]))
            emit_col_bcast_scalar(p, m11[0:1, 1:2], rs1[:, 1:2], onesr)
        w2cl.append(c_fin)

        for jdx in range(NF):
            def c_t(j=jdx):
                sl = stage[:, (j % 2) * 2048:(j % 2) * 2048 + 1024]
                qn = "qpa" if j % 2 == 0 else "qpb"
                w = [("v", p.c("v"))] + ([] if j >= 2 else list(p.dma_totals()))
                q = p.ld(sl, w2t_d[j * 128:(j + 1) * 128, :], wait=w, q=qn)
                p.v(lambda e, sl=sl: e.tensor_scalar(
                    out=d1[:, :D], in0=sl, scalar1=rs1[:, 1:2], scalar2=0.5,
                    op0=ALU.mult, op1=ALU.is_ge), wait=[(qn, q)])
                p.v(lambda e, sl=sl: e.tensor_scalar(
                    out=d2[:, :D], in0=sl, scalar1=rs1[:, 1:2], scalar2=-0.5,
                    op0=ALU.mult, op1=ALU.is_le))
                p.v(lambda e, j=j: e.tensor_sub(
                    out=w2tern[:, j * D:(j + 1) * D], in0=d1[:, :D],
                    in1=d2[:, :D]))
            w2cl.append(c_t)
        p.bgq = w2cl

        setup_v = p.c("v")
        setup_a = p.c("a")
        setup_qg = p.c("qg")
        last_store = [0]
        h_evac = {0: 0, 1: 0}
        y_evac = {0: 0, 1: 0}
        w1_ld = {}
        for c4 in range(4):
            for i in range(NF):
                wld = [("qg", setup_qg)]
                if (i % 2) in w1_ld and w1_ld[i % 2]:
                    wld.append(("t", w1_ld[i % 2]))
                qn = "ql%d_%d" % (i % 2, c4 % 2)
                q = p.ld(w1sl[:, (i % 2) * 1024:((i % 2) + 1) * 1024],
                         w1scr[i * 128:(i + 1) * 128, :], wait=wld, q=qn)
                ps = ps_h if i % 2 == 0 else ps_h2
                for k in range(ND):
                    w = [(qn, q)] if k == 0 else []
                    if k == 0:
                        w.append(("a", max(setup_a, h_evac[i % 2])))
                        if i == 0 and c4 == 0:
                            w.append(("v", setup_v))
                    p.t(lambda e, i=i, k=k, c4=c4, ps=ps: e.matmul(
                        ps[:, :512],
                        w1sl[:, (i % 2) * 1024 + k * 128:(i % 2) * 1024 + (k + 1) * 128],
                        x_fm[:, k * T + c4 * 512: k * T + (c4 + 1) * 512],
                        start=(k == 0), stop=(k == ND - 1)), wait=w)
                w1_ld[i % 2] = p.c("t")
                tc = p.c("t")
                h_evac[i % 2] = p.s(lambda e, i=i, ps=ps: e.activation(
                    out=hbuf[:, i * 512:(i + 1) * 512], in_=ps[:, :512],
                    func=AF.Gelu, bias=b1f[:, i:i + 1]), wait=[("t", tc)])
                p.pump(3)
            h_done = p.c("a")
            if c4 == 0:
                p.flush_bg()
                w2_done = p.c("v")
            for ts in range(4):
                c = c4 * 4 + ts
                qrn = "qr%d" % (ts % 2)
                mxr = max(y_evac[0], y_evac[1])
                rq = p.ld(xres[:, (ts % 2) * D:((ts % 2) + 1) * D],
                          x_in[c * 128:(c + 1) * 128, :],
                          wait=([("v", mxr if mxr else p.c("v"))]
                                if (c4 or ts) else []), q=qrn)
                for nn in range(2):
                    ps = ps_y if nn == 0 else ps_y2
                    wb = [("a", h_done)]
                    if y_evac[nn]:
                        wb.append(("v", y_evac[nn]))
                    elif c4 == 0 and ts == 0:
                        wb.append(("v", w2_done))
                    p.t(lambda e, nn=nn, ps=ps: e.matmul(
                        ps[:, :512], onesb[:, :], b2b[:, nn * 512:(nn + 1) * 512],
                        start=True, stop=False), wait=wb)
                    for i in range(NF):
                        p.t(lambda e, i=i, ts=ts, nn=nn, ps=ps: e.matmul(
                            ps[:, :512],
                            hbuf[:, i * 512 + ts * 128: i * 512 + (ts + 1) * 128],
                            w2tern[:, i * D + nn * 512: i * D + (nn + 1) * 512],
                            start=False, stop=(i == NF - 1)))
                    tc = p.c("t")
                    y_evac[nn] = p.v(lambda e, ts=ts, nn=nn, ps=ps: e.tensor_add(
                        out=pre[:, nn * 512:(nn + 1) * 512], in0=ps[:, :512],
                        in1=xres[:, (ts % 2) * D + nn * 512:(ts % 2) * D + (nn + 1) * 512]),
                        wait=[("t", tc), (qrn, rq)])
                emit_ln_tile(p, c, pre[:, :D], x_out, scr, stats, cst[:, 0:1],
                             wait=[("v", p.c("v"))], last_store=last_store)




def emit_l0ffn(p, dtf, x_in, x_out, ext=None):
    if True:
        nc = p.nc
        ctx = p.ctx
        ident = dtf("ident", [128, 128], BF16, kind="ExternalInput").ap()
        identf_d = dtf("identf", [128, 128], F32, kind="ExternalInput").ap()
        ones_row = dtf("ones_row", [1, 128], F32, kind="ExternalInput").ap()
        if ext is None:
            w1t_d = dtf("w1t", [D, FF], F32, kind="ExternalInput").ap()
            w2t_d = dtf("w2t", [FF, D], F32, kind="ExternalInput").ap()
            w1scr = dtf("w1scr", [NF * 128, 1024], BF16, kind="Internal").ap()
        else:
            w1scr = ext["w1scr"]
        cw_d = dtf("cw", [128, 3 * NF], F32, kind="ExternalInput").ap()
        cb_d = dtf("cb", [128, NF], F32, kind="ExternalInput").ap()
        rowscr = dtf("rowscr", [16, 128], F32, kind="Internal").ap()

        identb = p.sbuf("identb", [128, 128], BF16)
        identf = p.sbuf("identf_s", [128, 128], F32)
        onesr = p.sbuf("onesr", [1, 128], F32)
        onesc = p.sbuf("onesc", [128, 1], F32)
        cwf = p.sbuf("cwf", [128, 3 * NF], F32)
        cbf = p.sbuf("cbf", [128, NF], F32)
        cst = p.sbuf("cst", [128, 4], F32)
        p.ld(identb[:], ident)
        p.ld(identf[:], identf_d)
        p.ld(onesr[:], ones_row)
        p.ld(cwf[:], cw_d)
        p.ld(cbf[:], cb_d)
        c0 = p.c("qs")
        p.v(lambda e: e.memset(cst[:, 0:1], 1e-5), wait=[("qs", c0)])
        p.v(lambda e: e.memset(onesc[:], 1.0))

        xq_fm = p.sbuf("xq_fm", [128, ND * T], BF16)
        w2tern = p.sbuf("w2tern", [128, NF * D], BF16)
        hbuf = p.sbuf("hbuf", [128, NF * 516], BF16)
        stage = p.sbuf("stage", [128, 2 * 1024], F32)
        w1sl = p.sbuf("w1sl", [128, 2 * 1024], BF16)
        xst = p.sbuf("xst", [128, 2 * D], F32)
        qscr = p.sbuf("qscr", [128, D], BF16)
        q32 = p.sbuf("q32", [128, 1024], I32)
        ub = p.sbuf("ub", [128, 1024], F32)
        e1b = p.sbuf("e1b", [128, 1024], F32)
        vb = p.sbuf("vb", [128, 1024], F32)
        d1 = ub
        d2 = e1b
        amax = p.sbuf("amax", [128, NT], F32)
        rsc = p.sbuf("rsc", [128, NT], F32)
        sxiv = p.sbuf("sxiv", [128, NT], F32)
        a2 = p.sbuf("a2", [128, NT], F32)
        rs2 = p.sbuf("rs2", [128, NT], F32)
        s2iv = p.sbuf("s2iv", [128, NT], F32)
        rowbuf = p.sbuf("rowbuf", [16, 128], F32)
        rows = p.sbuf("rows", [1, T], F32)
        dq1 = p.sbuf("dq1", [128, T], F32)
        sq2 = p.sbuf("sq2", [128, 512], F32)
        acc = p.sbuf("acc", [128, 2], F32)
        m11 = p.sbuf("m11", [1, 4], F32)
        rs1 = p.sbuf("rs1", [128, 4], F32)
        xres = xst
        pre = p.sbuf("pre", [128, D], F32)
        scr = p.sbuf("scr", [128, D], F32)
        stats = p.sbuf("stats", [128, 16], F32)

        ps_h = p.psum("ps_h", [128, 512], F32)
        ps_h2 = p.psum("ps_h2", [128, 512], F32)
        ps_hb = p.psum("ps_hb", [128, 8], F32)
        ps_y = p.psum("ps_y", [128, 512], F32)
        ps_y2 = p.psum("ps_y2", [128, 512], F32)
        ps_tr = p.psum("ps_tr", [128, 512], BF16)
        ps_trf = p.psum("ps_trf", [128, 512], F32)
        ps_bc = p.psum("ps_bc", [128, 512], F32)
        p.buf["ps_sc"] = ps_hb
        p.buf["ps_sc2"] = ps_bc

        ux = p.uid()
        prior = p.dma_totals()
        for c in range(NT):
            sl = xst[:, (c % 2) * D:((c % 2) + 1) * D]
            qn = "qx%d_%d" % (c % 2, ux)
            w = list(prior) if c < 2 else [("v", p.c("v"))]
            q = p.ld(sl, x_in[c * 128:(c + 1) * 128, :], wait=w, q=qn)
            p.v(lambda e, sl=sl, c=c: e.reduce_max(
                out=amax[:, c:c + 1], in_=sl, axis=AX,
                apply_absolute_value=True), wait=[(qn, q)])
            p.v(lambda e, c=c: e.tensor_scalar(
                out=amax[:, c:c + 1], in0=amax[:, c:c + 1], scalar1=1e-5,
                scalar2=None, op0=ALU.max))
            p.v(lambda e, c=c: e.reciprocal(out=rsc[:, c:c + 1],
                                            in_=amax[:, c:c + 1]))
            p.v(lambda e, c=c: e.tensor_scalar(
                out=rsc[:, c:c + 1], in0=rsc[:, c:c + 1], scalar1=127.0,
                scalar2=None, op0=ALU.mult))
            p.v(lambda e, c=c: e.tensor_scalar(
                out=sxiv[:, c:c + 1], in0=amax[:, c:c + 1], scalar1=1.0 / 127.0,
                scalar2=None, op0=ALU.mult))
            p.v(lambda e, sl=sl, c=c: e.tensor_scalar(
                out=q32[:, :D], in0=sl, scalar1=rsc[:, c:c + 1], scalar2=None,
                op0=ALU.mult), wait=[("t", p.c("t"))] if c else [])
            p.v(lambda e, sl=sl, c=c: e.scalar_tensor_tensor(
                out=ub[:, :D], in0=sl, scalar=rsc[:, c:c + 1], in1=q32[:, :D],
                op0=ALU.mult, op1=ALU.subtract))
            p.v(lambda e: e.scalar_tensor_tensor(
                out=e1b[:, :D], in0=ub[:, :D], scalar=0.5, in1=q32[:, :D],
                op0=ALU.is_ge, op1=ALU.add))
            vq = p.v(lambda e: e.scalar_tensor_tensor(
                out=qscr[:, :D], in0=ub[:, :D], scalar=-0.5, in1=e1b[:, :D],
                op0=ALU.is_le, op1=ALU.subtract))
            for f0 in range(0, ND, 4):
                wv = [("v", vq), ("a", p.c("a"))]
                for k in range(4):
                    p.t(lambda e, f0=f0, k=k: e.transpose(
                        ps_tr[:, k * 128:(k + 1) * 128],
                        qscr[:, (f0 + k) * 128:(f0 + k + 1) * 128],
                        identb[:]), wait=(wv if k == 0 else []))
                tc = p.c("t")
                for k in range(4):
                    p.s(lambda e, f0=f0, k=k, c=c: e.activation(
                        out=xq_fm[:, (f0 + k) * T + c * 128:(f0 + k) * T + (c + 1) * 128],
                        in_=ps_tr[:, k * 128:(k + 1) * 128], func=AF.Copy),
                        wait=[("t", tc)])
        vv = p.c("v")
        tsx = p.t(lambda e: e.transpose(ps_trf[:16, :128], sxiv[:, :NT], identf[:]),
                  wait=[("v", vv), ("a", p.c("a"))])
        p.v(lambda e: e.tensor_copy(out=rowbuf[:16, :].rearrange("a b -> b a"),
                                    in_=ps_trf[:16, :128]) if False else
            e.tensor_copy(out=rowbuf[:16, :128], in_=ps_trf[:16, :128]),
            wait=[("t", tsx)])
        qrow0 = p.ldg(rowscr[:, :], rowbuf[:16, :128],
                      wait=[("v", p.c("v"))], q="qrow")
        qrow = p.ldg(rows[0:1, 0:T], rowscr.rearrange("a b -> () (a b)"),
                     wait=[("qrow", qrow0)], q="qrow")

        w2_ld = 0
        if ext is None:
            w1v = w1t_d.rearrange("a (b c) -> (a b) c", b=4)
            w2v = w2t_d.rearrange("a (b c) -> (a b) c", b=2)
            emit_absmean_stream(p, w1v, 4 * D, 1024, stage, acc, onesc,
                                m11[0:1, 0:1], 1e-5)
            p.v(lambda e: e.reciprocal(out=m11[0:1, 1:2], in_=m11[0:1, 0:1]))
            emit_col_bcast_scalar(p, m11[0:1, 1:2], rs1[:, 0:1], onesr)
            p.v(lambda e: e.tensor_scalar(
                out=rows[0:1, 0:T], in0=rows[0:1, 0:T], scalar1=m11[0:1, 0:1],
                scalar2=-1.0, op0=ALU.mult, op1=ALU.mult), wait=[("qrow", qrow)])
            emit_bcast_row(p, rows[0:1, 0:T], dq1, T, onesr)

            uq = p.uid()
            prior = p.dma_totals()
            for k in range(ND):
                for hh in range(4):
                    sl = stage[:, (hh % 2) * 1024:((hh % 2) + 1) * 1024]
                    qn = "qw%d_%d" % (hh % 2, uq)
                    w = [("qg", p.c("qg"))] + (list(prior) if k == 0 else [])
                    if p.c("v"):
                        w.append(("v", p.c("v")))
                    q = p.ld(sl, w1t_d[k * 128:(k + 1) * 128, hh * 1024:(hh + 1) * 1024],
                             wait=w, q=qn)
                    p.v(lambda e, sl=sl: e.tensor_scalar(
                        out=d1[:, :D], in0=sl, scalar1=rs1[:, 0:1], scalar2=0.5,
                        op0=ALU.mult, op1=ALU.is_ge), wait=[(qn, q)])
                    p.v(lambda e, sl=sl: e.tensor_scalar(
                        out=d2[:, :D], in0=sl, scalar1=rs1[:, 0:1], scalar2=-0.5,
                        op0=ALU.mult, op1=ALU.is_le))
                    vt3 = p.v(lambda e: e.tensor_sub(out=hbuf[:, :D], in0=d1[:, :D],
                                                     in1=d2[:, :D]))
                    for ii in range(8):
                        i = hh * 8 + ii
                        p.ldg(w1scr[i * 128:(i + 1) * 128, k * 128:(k + 1) * 128],
                              hbuf[:, ii * 128:(ii + 1) * 128], wait=[("v", vt3)])
            emit_absmean_stream(p, w2v, 2 * FF, 512, stage, acc, onesc,
                                m11[0:1, 2:3], 1e-5)
            p.v(lambda e: e.reciprocal(out=m11[0:1, 3:4], in_=m11[0:1, 2:3]))
            emit_col_bcast_scalar(p, m11[0:1, 3:4], rs1[:, 1:2], onesr)
            emit_col_bcast_scalar(p, m11[0:1, 2:3], rs1[:, 2:3], onesr)
            uq2 = p.uid()
            prior = p.dma_totals()
            for j in range(NF):
                sl = stage[:, (j % 2) * 1024:((j % 2) + 1) * 1024]
                qn = "qw%d_%d" % (j % 2, uq2)
                w = [("v", p.c("v"))] + ([] if j >= 2 else list(prior))
                q = p.ld(sl, w2t_d[j * 128:(j + 1) * 128, :], wait=w, q=qn)
                p.v(lambda e, sl=sl: e.tensor_scalar(
                    out=d1[:, :D], in0=sl, scalar1=rs1[:, 1:2], scalar2=0.5,
                    op0=ALU.mult, op1=ALU.is_ge), wait=[(qn, q)])
                p.v(lambda e, sl=sl: e.tensor_scalar(
                    out=d2[:, :D], in0=sl, scalar1=rs1[:, 1:2], scalar2=-0.5,
                    op0=ALU.mult, op1=ALU.is_le))
                p.v(lambda e, j=j: e.tensor_sub(
                    out=w2tern[:, j * D:(j + 1) * D], in0=d1[:, :D], in1=d2[:, :D]))
        else:
            mm_ld = p.ld(m11[0:1, 0:4], ext["mm_d"][0:1, 0:4], q="qrsl")
            rs_ld = p.ld(rs1[:, 0:4], ext["rs_d"][:, 0:4], q="qrsl")
            p.v(lambda e: e.tensor_scalar(
                out=rows[0:1, 0:T], in0=rows[0:1, 0:T], scalar1=m11[0:1, 0:1],
                scalar2=-1.0, op0=ALU.mult, op1=ALU.mult),
                wait=[("qrow", qrow), ("qrsl", rs_ld)])
            emit_bcast_row(p, rows[0:1, 0:T], dq1, T, onesr)
            for j in range(NF):
                w2_ld = p.ld(w2tern[:, j * D:(j + 1) * D],
                             ext["w2scr"][j * 128:(j + 1) * 128, :], q="qw2l")

        setup_v = p.c("v")
        setup_a = p.c("a")
        setup_qg = p.c("qg")
        last_store = [0]
        h_evac = {0: 0, 1: 0}
        hb_evac = [0]
        y_evac = {0: 0, 1: 0}
        w1_ld = {0: 0, 1: 0}
        for q4 in range(4):
            t0 = q4 * 512
            lo = max(t0 - 1, 0)
            hi = min(t0 + 513, T)
            nw = hi - lo
            off = 1 - (t0 - lo)
            for i in range(NF):
                qn = "ql%d" % (i % 2)
                wld = [("qg", setup_qg)]
                if w1_ld[i % 2]:
                    wld.append(("t", w1_ld[i % 2]))
                qw = p.ld(w1sl[:, (i % 2) * 1024:((i % 2) + 1) * 1024],
                          w1scr[i * 128:(i + 1) * 128, :], wait=wld, q=qn)
                ps = ps_h if i % 2 == 0 else ps_h2
                nb = nw - 512
                for k in range(ND):
                    w = [(qn, qw)] if k == 0 else []
                    if k == 0:
                        w.append(("v", max(setup_v, h_evac[i % 2])))
                        if i == 0 and q4 == 0:
                            w.append(("a", setup_a))
                    p.t(lambda e, i=i, k=k, ps=ps, lo=lo: e.matmul(
                        ps[:, :512],
                        w1sl[:, (i % 2) * 1024 + k * 128:(i % 2) * 1024 + (k + 1) * 128],
                        xq_fm[:, k * T + lo: k * T + lo + 512],
                        start=(k == 0), stop=(k == ND - 1)), wait=w)
                for k in range(ND):
                    wb2 = [("v", hb_evac[0])] if (k == 0 and hb_evac[0]) else []
                    p.t(lambda e, i=i, k=k, lo=lo, nb=nb: e.matmul(
                        ps_hb[:, :nb],
                        w1sl[:, (i % 2) * 1024 + k * 128:(i % 2) * 1024 + (k + 1) * 128],
                        xq_fm[:, k * T + lo + 512: k * T + lo + 512 + nb],
                        start=(k == 0), stop=(k == ND - 1)), wait=wb2)
                w1_ld[i % 2] = p.c("t")
                tc = p.c("t")
                p.v(lambda e, ps=ps, lo=lo: e.tensor_mul(
                    out=vb[:, :512], in0=ps[:, :512], in1=dq1[:, lo:lo + 512]),
                    wait=[("t", tc)])
                hb_evac[0] = p.v(lambda e, lo=lo, nb=nb: e.tensor_mul(
                    out=vb[:, 512:512 + nb], in0=ps_hb[:, :nb],
                    in1=dq1[:, lo + 512:lo + 512 + nb]))
                if q4 == 0:
                    p.v(lambda e, i=i: e.memset(hbuf[:, i * 516:i * 516 + 1], 0.0))
                if q4 == 3:
                    p.v(lambda e, i=i, off=off, nw=nw: e.memset(
                        hbuf[:, i * 516 + off + nw:i * 516 + off + nw + 1], 0.0))
                h_evac[i % 2] = p.v(lambda e, i=i, off=off, nw=nw: e.tensor_copy(
                    out=hbuf[:, i * 516 + off: i * 516 + off + nw], in_=vb[:, :nw]))
            hq_done = p.c("v")
            for i in range(NF):
                base = i * 516
                p.v(lambda e, i=i, base=base: e.tensor_scalar(
                    out=vb[:, :512], in0=hbuf[:, base:base + 512],
                    scalar1=cwf[:, i:i + 1], scalar2=None, op0=ALU.mult),
                    wait=[("v", hq_done)] if i == 0 else [])
                p.v(lambda e, i=i, base=base: e.scalar_tensor_tensor(
                    out=vb[:, :512], in0=hbuf[:, base + 1:base + 513],
                    scalar=cwf[:, NF + i:NF + i + 1], in1=vb[:, :512],
                    op0=ALU.mult, op1=ALU.add))
                cv = p.v(lambda e, i=i, base=base: e.scalar_tensor_tensor(
                    out=vb[:, :512], in0=hbuf[:, base + 2:base + 514],
                    scalar=cwf[:, 2 * NF + i:2 * NF + i + 1], in1=vb[:, :512],
                    op0=ALU.mult, op1=ALU.add))
                sv = p.s(lambda e, i=i, base=base: e.activation(
                    out=hbuf[:, base + 2:base + 514], in_=vb[:, :512],
                    func=AF.Silu, bias=cbf[:, i:i + 1]), wait=[("v", cv)])
                if i == 0:
                    p.v(lambda e, base=base: e.tensor_copy(
                        out=d1[:, :512], in_=hbuf[:, base + 2:base + 514]),
                        wait=[("a", sv)])
                    p.v(lambda e, base=base: e.tensor_copy(
                        out=d2[:, :512], in_=hbuf[:, base + 2:base + 514]))
                else:
                    p.v(lambda e, base=base: e.tensor_tensor(
                        out=d1[:, :512], in0=hbuf[:, base + 2:base + 514],
                        in1=d1[:, :512], op=ALU.max), wait=[("a", sv)])
                    p.v(lambda e, base=base: e.tensor_tensor(
                        out=d2[:, :512], in0=hbuf[:, base + 2:base + 514],
                        in1=d2[:, :512], op=ALU.min))
            p.v(lambda e: e.tensor_scalar(
                out=d2[:, :512], in0=d2[:, :512], scalar1=-1.0, scalar2=None,
                op0=ALU.mult))
            p.v(lambda e: e.tensor_max(out=d1[:, :512], in0=d1[:, :512],
                                       in1=d2[:, :512]))
            amx_v = p.c("v")
            for b4 in range(4):
                tq = p.t(lambda e, b4=b4: e.transpose(
                    ps_trf[:, b4 * 128:(b4 + 1) * 128],
                    d1[:, b4 * 128:(b4 + 1) * 128], identf[:]),
                    wait=[("v", amx_v)] if b4 == 0 else [])
            tc = p.c("t")
            for b4 in range(4):
                c = q4 * 4 + b4
                p.v(lambda e, b4=b4, c=c: e.reduce_max(
                    out=a2[:, c:c + 1], in_=ps_trf[:, b4 * 128:(b4 + 1) * 128],
                    axis=AX), wait=[("t", tc)] if b4 == 0 else [])
                p.v(lambda e, c=c: e.tensor_scalar(
                    out=a2[:, c:c + 1], in0=a2[:, c:c + 1], scalar1=1e-5,
                    scalar2=None, op0=ALU.max))
                p.v(lambda e, c=c: e.reciprocal(out=rs2[:, c:c + 1],
                                                in_=a2[:, c:c + 1]))
                p.v(lambda e, c=c: e.tensor_scalar(
                    out=rs2[:, c:c + 1], in0=rs2[:, c:c + 1], scalar1=127.0,
                    scalar2=None, op0=ALU.mult))
                p.v(lambda e, c=c: e.tensor_scalar(
                    out=s2iv[:, c:c + 1], in0=a2[:, c:c + 1], scalar1=1.0 / 127.0,
                    scalar2=None, op0=ALU.mult))
            vv2 = p.c("v")
            tq2 = p.t(lambda e, q4=q4: e.transpose(
                ps_trf[:4, :128], rs2[:, q4 * 4:q4 * 4 + 4], identf[:]),
                wait=[("v", vv2)])
            p.v(lambda e: e.tensor_copy(out=rowbuf[:4, :128], in_=ps_trf[:4, :128]),
                wait=[("t", tq2)])
            qrow2a = p.ldg(rowscr[0:4, :], rowbuf[:4, :128],
                           wait=[("v", p.c("v"))], q="qrow")
            qrow2 = p.ldg(rows[0:1, 0:512],
                          rowscr[0:4, :].rearrange("a b -> () (a b)"),
                          wait=[("qrow", qrow2a)], q="qrow")
            emit_bcast_row(p, rows[0:1, 0:512], sq2, 512, onesr,
                           wait0=[("qrow", qrow2)])
            sq_done = p.c("v")
            for i in range(NF):
                base = i * 516
                p.v(lambda e, base=base: e.tensor_mul(
                    out=vb[:, :512], in0=hbuf[:, base + 2:base + 514],
                    in1=sq2[:, :512]), wait=[("v", sq_done)] if i == 0 else [])
                p.v(lambda e: e.tensor_copy(out=q32[:, :512], in_=vb[:, :512]))
                p.v(lambda e: e.tensor_sub(out=ub[:, :512], in0=vb[:, :512],
                                           in1=q32[:, :512]))
                p.v(lambda e: e.scalar_tensor_tensor(
                    out=e1b[:, :512], in0=ub[:, :512], scalar=0.5,
                    in1=q32[:, :512], op0=ALU.is_ge, op1=ALU.add))
                p.v(lambda e, base=base: e.scalar_tensor_tensor(
                    out=hbuf[:, base + 2:base + 514], in0=ub[:, :512],
                    scalar=-0.5, in1=e1b[:, :512], op0=ALU.is_le,
                    op1=ALU.subtract))
            hq2_done = p.c("v")
            for b4 in range(4):
                c = q4 * 4 + b4
                p.v(lambda e, c=c: e.tensor_scalar(
                    out=s2iv[:, c:c + 1], in0=s2iv[:, c:c + 1],
                    scalar1=rs1[:, 2:3], scalar2=-1.0, op0=ALU.mult,
                    op1=ALU.mult))
            sc_done = p.c("v")
            for b4 in range(4):
                c = q4 * 4 + b4
                qrn = "qr%d" % (b4 % 2)
                mxr = max(y_evac[0], y_evac[1])
                rq = p.ld(xres[:, (b4 % 2) * D:((b4 % 2) + 1) * D],
                          x_in[c * 128:(c + 1) * 128, :],
                          wait=[("v", mxr if mxr else p.c("v"))] +
                               (p.dma_totals() if (q4 == 0 and b4 < 2) else []),
                          q=qrn)
                for nn in range(2):
                    ps = ps_y if nn == 0 else ps_y2
                    for i in range(NF):
                        w = []
                        if i == 0:
                            w = [("v", max(hq2_done, sc_done, y_evac[nn]))]
                            if w2_ld and q4 == 0 and b4 == 0:
                                w.append(("qw2l", w2_ld))
                        p.t(lambda e, i=i, b4=b4, nn=nn, ps=ps: e.matmul(
                            ps[:, :512],
                            hbuf[:, i * 516 + 2 + b4 * 128: i * 516 + 2 + (b4 + 1) * 128],
                            w2tern[:, i * D + nn * 512: i * D + (nn + 1) * 512],
                            start=(i == 0), stop=(i == NF - 1)), wait=w)
                    tc = p.c("t")
                    y_evac[nn] = p.v(lambda e, c=c, nn=nn, ps=ps, b4=b4: (
                        e.scalar_tensor_tensor(
                            out=pre[:, nn * 512:(nn + 1) * 512], in0=ps[:, :512],
                            scalar=s2iv[:, c:c + 1],
                            in1=xres[:, (b4 % 2) * D + nn * 512:(b4 % 2) * D + (nn + 1) * 512],
                            op0=ALU.mult, op1=ALU.add)),
                        wait=[("t", tc), (qrn, rq)])
                emit_ln_tile(p, c, pre[:, :D], x_out, scr, stats, cst[:, 0:1],
                             wait=[("v", p.c("v"))], last_store=last_store)




def emit_l2ffn(p, dtf, x_in, x_out, ext=None):
    if True:
        nc = p.nc
        ctx = p.ctx
        ident = dtf("ident", [128, 128], BF16, kind="ExternalInput").ap()
        ones_row = dtf("ones_row", [1, 128], F32, kind="ExternalInput").ap()
        if ext is None:
            wgt_d = dtf("wgt", [D, FF], F32, kind="ExternalInput").ap()
            wut_d = dtf("wut", [D, FF], F32, kind="ExternalInput").ap()
            wdt_d = dtf("wdt", [FF, D], F32, kind="ExternalInput").ap()
            wgscr = dtf("wgscr", [8 * 128, ND * 512], BF16, kind="Internal").ap()
            wuscr = dtf("wuscr", [8 * 128, ND * 512], BF16, kind="Internal").ap()
        else:
            wgscr = ext["wgscr"]
            wuscr = ext["wuscr"]

        identb = p.sbuf("identb", [128, 128], BF16)
        onesr = p.sbuf("onesr", [1, 128], F32)
        onesc = p.sbuf("onesc", [128, 1], F32)
        cst = p.sbuf("cst", [128, 4], F32)
        p.ld(identb[:], ident)
        p.ld(onesr[:], ones_row)
        c0 = p.c("qs")
        p.v(lambda e: e.memset(cst[:, 0:1], 1e-5), wait=[("qs", c0)])
        p.v(lambda e: e.memset(onesc[:], 1.0))

        wdtern = p.sbuf("wdtern", [128, NF * D], BF16)
        gu_h = p.sbuf("gu_h", [128, 8 * FF], BF16)
        xq_h = p.sbuf("xq_h", [128, ND * 1024], BF16)
        wgsl = p.sbuf("wgsl", [128, ND * 512], BF16)
        wusl = p.sbuf("wusl", [128, ND * 512], BF16)
        gqfm = p.sbuf("gqfm", [128, NF * 128], BF16)
        gtmp = p.sbuf("gtmp", [128, 512], F32)
        xst = p.sbuf("xst", [128, 2 * D], F32)
        stage = xst
        xres = p.sbuf("xres", [128, D], F32)
        qscr = p.sbuf("qscr", [128, D], BF16)
        q32 = p.sbuf("q32", [128, 1024], I32)
        ub = p.sbuf("ub", [128, 1024], F32)
        e1b = p.sbuf("e1b", [128, 1024], F32)
        d1 = ub
        d2 = e1b
        amax = p.sbuf("amax", [128, NT], F32)
        rsc = p.sbuf("rsc", [128, NT], F32)
        sxiv = p.sbuf("sxiv", [128, NT], F32)
        sg = p.sbuf("sg", [128, NT], F32)
        su = p.sbuf("su", [128, NT], F32)
        a2 = p.sbuf("a2", [128, NT], F32)
        rs2 = p.sbuf("rs2", [128, NT], F32)
        s2 = p.sbuf("s2", [128, NT], F32)
        acc = p.sbuf("acc", [128, 2], F32)
        m11 = p.sbuf("m11", [1, 8], F32)
        rs1 = p.sbuf("rs1", [128, 8], F32)
        pre = p.sbuf("pre", [128, D], F32)
        scr = p.sbuf("scr", [128, D], F32)
        stats = p.sbuf("stats", [128, 16], F32)

        ps_g = p.psum("ps_g", [128, 512], F32)
        ps_u = p.psum("ps_u", [128, 512], F32)
        ps_y = p.psum("ps_y", [128, 512], F32)
        ps_y2 = p.psum("ps_y2", [128, 512], F32)
        ps_tr = p.psum("ps_tr", [128, 512], BF16)
        ps_sc = p.psum("ps_sc", [1, 16], F32)
        ps_sc2 = p.psum("ps_sc2", [128, 16], F32)

        rs_ld = 0
        wdl_ld = 0
        if ext is None:
            wgv = wgt_d.rearrange("a (b c) -> (a b) c", b=4)
            wuv = wut_d.rearrange("a (b c) -> (a b) c", b=4)
            wdv = wdt_d.rearrange("a (b c) -> (a b) c", b=4)
            for wv_, nr, ncol, j in ((wgv, 4 * D, 1024, 0), (wuv, 4 * D, 1024, 1),
                                     (wdv, 4 * FF, 256, 2)):
                emit_absmean_stream(p, wv_, nr, ncol, stage, acc, onesc,
                                    m11[0:1, 2 * j:2 * j + 1], 1e-5)
                p.v(lambda e, j=j: e.reciprocal(out=m11[0:1, 2 * j + 1:2 * j + 2],
                                                in_=m11[0:1, 2 * j:2 * j + 1]))
                emit_col_bcast_scalar(p, m11[0:1, 2 * j + 1:2 * j + 2],
                                      rs1[:, j:j + 1], onesr)
                emit_col_bcast_scalar(p, m11[0:1, 2 * j:2 * j + 1],
                                      rs1[:, 3 + j:4 + j], onesr)

            for wsrc, wscr_, j in ((wgt_d, wgscr, 0), (wut_d, wuscr, 1)):
                uq = p.uid()
                prior = p.dma_totals()
                for k in range(ND):
                    for hh in range(4):
                        sl = stage[:, (hh % 2) * 1024:((hh % 2) + 1) * 1024]
                        qn = "qw%d_%d" % (hh % 2, uq)
                        w = [("qg", p.c("qg"))] + (list(prior) if (k == 0 and hh < 2) else [])
                        if p.c("v"):
                            w.append(("v", p.c("v")))
                        q = p.ld(sl, wsrc[k * 128:(k + 1) * 128, hh * 1024:(hh + 1) * 1024],
                                 wait=w, q=qn)
                        p.v(lambda e, sl=sl, j=j: e.tensor_scalar(
                            out=d1[:, :D], in0=sl, scalar1=rs1[:, j:j + 1], scalar2=0.5,
                            op0=ALU.mult, op1=ALU.is_ge), wait=[(qn, q)])
                        p.v(lambda e, sl=sl, j=j: e.tensor_scalar(
                            out=d2[:, :D], in0=sl, scalar1=rs1[:, j:j + 1], scalar2=-0.5,
                            op0=ALU.mult, op1=ALU.is_le))
                        vt3 = p.v(lambda e: e.tensor_sub(out=qscr[:, :D], in0=d1[:, :D],
                                                         in1=d2[:, :D]))
                        for bb in range(2):
                            fc = hh * 2 + bb
                            p.ldg(wscr_[fc * 128:(fc + 1) * 128, k * 512:(k + 1) * 512],
                                  qscr[:, bb * 512:(bb + 1) * 512], wait=[("v", vt3)])
            uq2 = p.uid()
            prior = p.dma_totals()
            for jj in range(NF):
                sl = stage[:, (jj % 2) * 1024:((jj % 2) + 1) * 1024]
                qn = "qw%d_%d" % (jj % 2, uq2)
                w = [("v", p.c("v"))] + ([] if jj >= 2 else list(prior))
                q = p.ld(sl, wdt_d[jj * 128:(jj + 1) * 128, :], wait=w, q=qn)
                p.v(lambda e, sl=sl: e.tensor_scalar(
                    out=d1[:, :D], in0=sl, scalar1=rs1[:, 2:3], scalar2=0.5,
                    op0=ALU.mult, op1=ALU.is_ge), wait=[(qn, q)])
                p.v(lambda e, sl=sl: e.tensor_scalar(
                    out=d2[:, :D], in0=sl, scalar1=rs1[:, 2:3], scalar2=-0.5,
                    op0=ALU.mult, op1=ALU.is_le))
                p.v(lambda e, jj=jj: e.tensor_sub(
                    out=wdtern[:, jj * D:(jj + 1) * D], in0=d1[:, :D], in1=d2[:, :D]))
        else:
            rs_ld = p.ld(rs1[:, 0:8], ext["rs_d"], q="qrsl")
            for jj in range(NF):
                wdl_ld = p.ld(wdtern[:, jj * D:(jj + 1) * D],
                              ext["wdscr"][jj * 128:(jj + 1) * 128, :], q="qwdl")

        last_store = [0]
        g_ev = [0]
        u_ev = [0]
        y_evac = {0: 0, 1: 0}
        wsl_t = {0: 0, 1: 0}
        for half in range(2):
            ux = p.uid()
            prior = p.dma_totals()
            for c8 in range(8):
                c = half * 8 + c8
                sl = xst[:, (c8 % 2) * D:((c8 % 2) + 1) * D]
                qn = "qx%d_%d" % (c8 % 2, ux)
                w = [("v", p.c("v"))] + (list(prior) if c8 < 2 else [])
                q = p.ld(sl, x_in[c * 128:(c + 1) * 128, :], wait=w, q=qn)
                p.v(lambda e, sl=sl, c=c: e.reduce_max(
                    out=amax[:, c:c + 1], in_=sl, axis=AX,
                    apply_absolute_value=True), wait=[(qn, q)])
                p.v(lambda e, c=c: e.tensor_scalar(
                    out=amax[:, c:c + 1], in0=amax[:, c:c + 1], scalar1=1e-5,
                    scalar2=None, op0=ALU.max))
                p.v(lambda e, c=c: e.reciprocal(out=rsc[:, c:c + 1],
                                                in_=amax[:, c:c + 1]))
                p.v(lambda e, c=c: e.tensor_scalar(
                    out=rsc[:, c:c + 1], in0=rsc[:, c:c + 1], scalar1=127.0,
                    scalar2=None, op0=ALU.mult))
                p.v(lambda e, c=c: e.tensor_scalar(
                    out=sxiv[:, c:c + 1], in0=amax[:, c:c + 1], scalar1=1.0 / 127.0,
                    scalar2=None, op0=ALU.mult))
                p.v(lambda e, c=c: e.tensor_scalar(
                    out=sg[:, c:c + 1], in0=sxiv[:, c:c + 1], scalar1=rs1[:, 3:4],
                    scalar2=-1.0, op0=ALU.mult, op1=ALU.mult),
                    wait=([("qrsl", rs_ld)] if (rs_ld and c == 0) else []))
                p.v(lambda e, c=c: e.tensor_scalar(
                    out=su[:, c:c + 1], in0=sxiv[:, c:c + 1], scalar1=rs1[:, 4:5],
                    scalar2=-1.0, op0=ALU.mult, op1=ALU.mult))
                p.v(lambda e, sl=sl, c=c: e.tensor_scalar(
                    out=q32[:, :D], in0=sl, scalar1=rsc[:, c:c + 1], scalar2=None,
                    op0=ALU.mult), wait=[("t", p.c("t"))])
                p.v(lambda e, sl=sl, c=c: e.scalar_tensor_tensor(
                    out=ub[:, :D], in0=sl, scalar=rsc[:, c:c + 1], in1=q32[:, :D],
                    op0=ALU.mult, op1=ALU.subtract))
                p.v(lambda e: e.scalar_tensor_tensor(
                    out=e1b[:, :D], in0=ub[:, :D], scalar=0.5, in1=q32[:, :D],
                    op0=ALU.is_ge, op1=ALU.add))
                vq = p.v(lambda e: e.scalar_tensor_tensor(
                    out=qscr[:, :D], in0=ub[:, :D], scalar=-0.5, in1=e1b[:, :D],
                    op0=ALU.is_le, op1=ALU.subtract))
                for f0 in range(0, ND, 4):
                    wv = [("v", vq), ("a", p.c("a"))]
                    for k in range(4):
                        p.t(lambda e, f0=f0, k=k: e.transpose(
                            ps_tr[:, k * 128:(k + 1) * 128],
                            qscr[:, (f0 + k) * 128:(f0 + k + 1) * 128],
                            identb[:]), wait=(wv if k == 0 else []))
                    tc = p.c("t")
                    for k in range(4):
                        p.s(lambda e, f0=f0, k=k, c8=c8: e.activation(
                            out=xq_h[:, (f0 + k) * 1024 + c8 * 128:(f0 + k) * 1024 + (c8 + 1) * 128],
                            in_=ps_tr[:, k * 128:(k + 1) * 128], func=AF.Copy),
                            wait=[("t", tc)])
            xq_done_a = p.c("a")
            xq_done_v = p.c("v")
            for fc in range(ND):
                qn = "qsl%d_%d" % (half, fc % 2)
                wld = [("qg", p.c("qg")), ("t", p.c("t"))]
                qw1 = p.ld(wgsl[:, :], wgscr[fc * 128:(fc + 1) * 128, :],
                           wait=wld, q=qn)
                qw2 = p.ld(wusl[:, :], wuscr[fc * 128:(fc + 1) * 128, :], q=qn)
                for c8 in range(8):
                    c = half * 8 + c8
                    wg_ = [(qn, qw2), ("a", max(xq_done_a, g_ev[0]))]
                    for k in range(ND):
                        p.t(lambda e, fc=fc, c8=c8, k=k: e.matmul(
                            ps_g[:, :512],
                            xq_h[:, k * 1024 + c8 * 128: k * 1024 + (c8 + 1) * 128],
                            wgsl[:, k * 512:(k + 1) * 512],
                            start=(k == 0), stop=(k == ND - 1)),
                            wait=(wg_ if k == 0 else []))
                    tcg = p.c("t")
                    g_ev[0] = p.s(lambda e, c=c: e.activation(
                        out=gtmp[:, :512], in_=ps_g[:, :512], func=AF.Silu,
                        scale=sg[:, c:c + 1]),
                        wait=[("t", tcg)] + ([("v", u_ev[0])] if u_ev[0] else []))
                    wu_ = [("v", u_ev[0])] if u_ev[0] else []
                    for k in range(ND):
                        p.t(lambda e, fc=fc, c8=c8, k=k: e.matmul(
                            ps_u[:, :512],
                            xq_h[:, k * 1024 + c8 * 128: k * 1024 + (c8 + 1) * 128],
                            wusl[:, k * 512:(k + 1) * 512],
                            start=(k == 0), stop=(k == ND - 1)),
                            wait=(wu_ if k == 0 else []))
                    tcu = p.c("t")
                    u_ev[0] = p.v(lambda e, c=c, c8=c8, fc=fc: e.scalar_tensor_tensor(
                        out=gu_h[:, c8 * FF + fc * 512: c8 * FF + (fc + 1) * 512],
                        in0=ps_u[:, :512], scalar=su[:, c:c + 1], in1=gtmp[:, :512],
                        op0=ALU.mult, op1=ALU.mult),
                        wait=[("t", tcu), ("a", g_ev[0])])
                wsl_t[fc % 2] = p.c("t")
            gu_done = p.c("v")
            for c8 in range(8):
                c = half * 8 + c8
                base = c8 * FF
                p.v(lambda e, base=base, c=c: e.reduce_max(
                    out=a2[:, c:c + 1], in_=gu_h[:, base:base + FF], axis=AX,
                    apply_absolute_value=True),
                    wait=[("v", gu_done)] if c8 == 0 else [])
                p.v(lambda e, c=c: e.tensor_scalar(
                    out=a2[:, c:c + 1], in0=a2[:, c:c + 1], scalar1=1e-5,
                    scalar2=None, op0=ALU.max))
                p.v(lambda e, c=c: e.reciprocal(out=rs2[:, c:c + 1],
                                                in_=a2[:, c:c + 1]))
                p.v(lambda e, c=c: e.tensor_scalar(
                    out=rs2[:, c:c + 1], in0=rs2[:, c:c + 1], scalar1=127.0,
                    scalar2=None, op0=ALU.mult))
                p.v(lambda e, c=c: e.tensor_scalar(
                    out=s2[:, c:c + 1], in0=a2[:, c:c + 1], scalar1=rs1[:, 5:6],
                    scalar2=-1.0 / 127.0, op0=ALU.mult, op1=ALU.mult))
                for ch in range(4):
                    sl0 = base + ch * 1024
                    p.v(lambda e, sl0=sl0, c=c: e.tensor_scalar(
                        out=q32[:, :D], in0=gu_h[:, sl0:sl0 + D],
                        scalar1=rs2[:, c:c + 1], scalar2=None, op0=ALU.mult))
                    p.v(lambda e, sl0=sl0, c=c: e.scalar_tensor_tensor(
                        out=ub[:, :D], in0=gu_h[:, sl0:sl0 + D],
                        scalar=rs2[:, c:c + 1], in1=q32[:, :D], op0=ALU.mult,
                        op1=ALU.subtract))
                    p.v(lambda e: e.scalar_tensor_tensor(
                        out=e1b[:, :D], in0=ub[:, :D], scalar=0.5, in1=q32[:, :D],
                        op0=ALU.is_ge, op1=ALU.add))
                    p.v(lambda e, sl0=sl0: e.scalar_tensor_tensor(
                        out=gu_h[:, sl0:sl0 + D], in0=ub[:, :D], scalar=-0.5,
                        in1=e1b[:, :D], op0=ALU.is_le, op1=ALU.subtract))
                vq2 = p.c("v")
                for f0 in range(0, NF, 4):
                    wv = [("v", vq2), ("a", p.c("a"))]
                    for k in range(4):
                        p.t(lambda e, f0=f0, k=k, base=base: e.transpose(
                            ps_tr[:, k * 128:(k + 1) * 128],
                            gu_h[:, base + (f0 + k) * 128: base + (f0 + k + 1) * 128],
                            identb[:]), wait=(wv if k == 0 else []))
                    tc = p.c("t")
                    for k in range(4):
                        p.s(lambda e, f0=f0, k=k: e.activation(
                            out=gqfm[:, (f0 + k) * 128:(f0 + k + 1) * 128],
                            in_=ps_tr[:, k * 128:(k + 1) * 128], func=AF.Copy),
                            wait=[("t", tc)])
                tq_done = p.c("a")
                qrn = "qr%d" % (c8 % 2)
                mxr = max(y_evac[0], y_evac[1])
                rq = p.ld(xres[:, :], x_in[c * 128:(c + 1) * 128, :],
                          wait=[("v", mxr if mxr else p.c("v"))], q=qrn)
                for nn in range(2):
                    ps = ps_y if nn == 0 else ps_y2
                    for i in range(NF):
                        w = []
                        if i == 0:
                            w = [("a", tq_done)]
                            if y_evac[nn]:
                                w.append(("v", y_evac[nn]))
                            if wdl_ld and half == 0 and c8 == 0:
                                w.append(("qwdl", wdl_ld))
                        p.t(lambda e, i=i, nn=nn, ps=ps: e.matmul(
                            ps[:, :512],
                            gqfm[:, i * 128:(i + 1) * 128],
                            wdtern[:, i * D + nn * 512: i * D + (nn + 1) * 512],
                            start=(i == 0), stop=(i == NF - 1)), wait=w)
                    tc = p.c("t")
                    y_evac[nn] = p.v(lambda e, c=c, nn=nn, ps=ps: (
                        e.scalar_tensor_tensor(
                            out=pre[:, nn * 512:(nn + 1) * 512], in0=ps[:, :512],
                            scalar=s2[:, c:c + 1],
                            in1=xres[:, nn * 512:(nn + 1) * 512],
                            op0=ALU.mult, op1=ALU.add)),
                        wait=[("t", tc), (qrn, rq)])
                emit_ln_tile(p, c, pre[:, :D], x_out, scr, stats, cst[:, 0:1],
                             wait=[("v", p.c("v"))], last_store=last_store)




def emit_attn(p, dtf, x_in, x_out, variant):
    if True:
        nc = p.nc
        ctx = p.ctx
        ident = dtf("ident", [128, 128], BF16, kind="ExternalInput").ap()
        ones_row = dtf("ones_row", [1, 128], F32, kind="ExternalInput").ap()
        onesb_d = dtf("ones_bf", [1, 128], BF16, kind="ExternalInput").ap()
        onescb_d = dtf("onescb", [128, 1], BF16, kind="ExternalInput").ap()
        masks_d = dtf("masks", [4 * 128, 512], BF16, kind="ExternalInput").ap()
        if variant == "l1":
            wqkv_d = dtf("wqkv", [D, 3 * D], F32, kind="ExternalInput").ap()
            wo_d = dtf("wo", [D, D], F32, kind="ExternalInput").ap()
            battn_d = dtf("battn", [128, 24], F32, kind="ExternalInput").ap()
            battnr_d = dtf("battnr", [1, D], F32, kind="ExternalInput").ap()
            bproj_d = dtf("bprojr", [1, D], F32, kind="ExternalInput").ap()
        else:
            wq_d = dtf("wqT", [D, D], F32, kind="ExternalInput").ap()
            wk_d = dtf("wkT", [D, D], F32, kind="ExternalInput").ap()
            wv_d = dtf("wvT", [D, D], F32, kind="ExternalInput").ap()
            wo_d = dtf("woT", [D, D], F32, kind="ExternalInput").ap()
        if variant == "l0":
            ropec_d = dtf("ropec", [128, T], F32, kind="ExternalInput").ap()
            ropes_d = dtf("ropes", [128, T], F32, kind="ExternalInput").ap()

        identb = p.sbuf("identb", [128, 128], BF16)
        onesr = p.sbuf("onesr", [1, 128], F32)
        onesbr = p.sbuf("onesbr", [1, 128], BF16)
        onescb = p.sbuf("onescb_s", [128, 1], BF16)
        if variant != "l0":
            maskt = p.sbuf("maskt", [128, 4 * 512], BF16)
        cst = p.sbuf("cst", [128, 4], F32)
        p.ld(identb[:], ident)
        p.ld(onesr[:], ones_row)
        p.ld(onesbr[:], onesb_d)
        p.ld(onescb[:], onescb_d)
        if variant != "l0":
            for i in range(4):
                p.ld(maskt[:, i * 512:(i + 1) * 512],
                     masks_d[i * 128:(i + 1) * 128, :])
        c0 = p.c("qs")
        p.v(lambda e: e.memset(cst[:, 0:1], 1e-5), wait=[("qs", c0)])

        x_fm = p.sbuf("x_fm", [128, ND * T], BF16)
        attn_fm = x_fm
        q_fm = p.sbuf("q_fm", [128, ND * T], BF16)
        k_fm = p.sbuf("k_fm", [128, ND * T], BF16)
        v_tm = p.sbuf("v_tm", [128, NT * D], BF16)
        if variant == "l1":
            wqkv = p.sbuf("wqkv_s", [128, ND * 3 * D], BF16)
            bfm = p.sbuf("bfm", [128, 24], F32)
            brophalf = p.sbuf("bqkvb", [1, D], BF16)
            bprowb = p.sbuf("bprowb", [1, D], BF16)
        else:
            wqb = p.sbuf("wqb", [128, ND * D], BF16)
            wkb = p.sbuf("wkb", [128, ND * D], BF16)
            wvb = wkb
            wob = wqb
        if variant == "l0":
            ropec = p.sbuf("ropec_s", [128, T], BF16)
            ropes = p.sbuf("ropes_s", [128, T], BF16)
            rt1 = None
            rt2 = None
        et0 = p.sbuf("et0", [128, 512], BF16)
        et1 = p.sbuf("et1", [128, 512], BF16)
        et2 = p.sbuf("et2", [128, 512], BF16)
        et3 = p.sbuf("et3", [128, 512], BF16)
        den = p.sbuf("den", [1, 512], F32)
        rbc = p.sbuf("rbc", [64, 512], F32)
        xst = p.sbuf("xst", [128, 2 * D], BF16)
        xres = p.sbuf("xres", [128, D], F32)
        pre = xres if variant == "l1" else p.sbuf("pre", [128, D], F32)
        scr = p.sbuf("scr", [128, D], F32)
        stats = p.sbuf("stats", [128, 16], F32)

        ps_s = p.psum("ps_s", [128, 512], F32)
        ps_s2 = p.psum("ps_s2", [128, 512], F32)
        ps_pv = p.psum("ps_pv", [64, 512], F32)
        ps_den = p.psum("ps_den", [1, 512], F32)
        ps_bc = p.psum("ps_bc", [64, 512], F32)
        ps_tr = p.psum("ps_tr", [128, 512], BF16)
        ps_y = p.psum("ps_y", [128, 512], F32)
        ps_y2 = p.psum("ps_y2", [128, 512], F32)

        if p.bgq_init is not None:
            p.bgq = p.bgq_init()
            p.bgq_init = None

        if variant == "l1":
            for k in range(ND):
                p.ldg(wqkv[:, k * 3 * D:(k + 1) * 3 * D],
                      wqkv_d[k * 128:(k + 1) * 128, :])
            p.ld(bfm[:], battn_d, q="qb1")
            bb0 = p.ldg(brophalf[:], battnr_d, q="qb1g")
            bb1 = p.ldg(bprowb[:], bproj_d, q="qb2")
        else:
            vv0 = vv1 = 0
            for k in range(ND):
                p.ldg(wqb[:, k * D:(k + 1) * D], wq_d[k * 128:(k + 1) * 128, :])
                p.ldg(wkb[:, k * D:(k + 1) * D], wk_d[k * 128:(k + 1) * 128, :])
        if variant == "l0":
            p.ldg(ropec[:], ropec_d, q="qrope")
            p.ldg(ropes[:], ropes_d, q="qrope")
        wload = p.c("qg")

        emit_stream_x_fm(p, x_in, x_fm, xst, identb,
                         wait0=[("qs", p.c("qs"))])
        xfm_t = p.c("t")
        xfm_a = p.c("a")

        def q_lhsT(k, oc):
            if variant == "l1":
                return wqkv[:, k * 3 * D + oc * 128: k * 3 * D + (oc + 1) * 128]
            return wqb[:, k * D + oc * 128: k * D + (oc + 1) * 128]

        def k_lhsT(k, oc):
            if variant == "l1":
                return wqkv[:, k * 3 * D + D + oc * 128: k * 3 * D + D + (oc + 1) * 128]
            return wkb[:, k * D + oc * 128: k * D + (oc + 1) * 128]

        def v_rhs(k, dvc):
            if variant == "l1":
                return wqkv[:, k * 3 * D + 2 * D + dvc * 512: k * 3 * D + 2 * D + (dvc + 1) * 512]
            return wvb[:, k * D + dvc * 512: k * D + (dvc + 1) * 512]

        evac = {0: 0, 1: 0}
        for dest, lhsT_fn, bcol in ((q_fm, q_lhsT, 0), (k_fm, k_lhsT, 8)):
            for oc in range(ND):
                for tc4 in range(4):
                    par = tc4 % 2
                    ps = ps_s if par == 0 else ps_s2
                    for k in range(ND):
                        w = []
                        if k == 0:
                            w = [("a", max(xfm_a, evac[par])), ("qg", wload)]
                        p.t(lambda e, oc=oc, tc4=tc4, k=k, ps=ps,
                            lhsT_fn=lhsT_fn: e.matmul(
                            ps[:, :512], lhsT_fn(k, oc),
                            x_fm[:, k * T + tc4 * 512:(k) * T + (tc4 + 1) * 512],
                            start=(k == 0), stop=(k == ND - 1)), wait=w)
                    tc = p.c("t")
                    if variant == "l1":
                        evac[par] = p.s(lambda e, oc=oc, tc4=tc4, ps=ps,
                                        bcol=bcol, dest=dest: e.activation(
                            out=dest[:, oc * T + tc4 * 512: oc * T + (tc4 + 1) * 512],
                            in_=ps[:, :512], func=AF.Identity,
                            bias=bfm[:, bcol + oc:bcol + oc + 1]),
                            wait=[("t", tc), ("qb1", 16)])
                    else:
                        evac[par] = p.s(lambda e, oc=oc, tc4=tc4, ps=ps,
                                        dest=dest: e.activation(
                            out=dest[:, oc * T + tc4 * 512: oc * T + (tc4 + 1) * 512],
                            in_=ps[:, :512], func=AF.Copy), wait=[("t", tc)])
                    p.pump(1)
        wv_ld = 0
        if variant != "l1":
            uv = p.uid()
            for k in range(ND):
                p.ldg(wvb[:, k * D:(k + 1) * D], wv_d[k * 128:(k + 1) * 128, :],
                      wait=([("t", p.c("t"))] if k == 0 else []),
                      q="qwv_%d" % uv)
            wv_ld = p.c("qwv_%d" % uv)
            qwv = "qwv_%d" % uv
        for c in range(NT):
            for dvc in range(2):
                par = dvc % 2
                ps = ps_s if par == 0 else ps_s2
                if variant == "l1":
                    p.t(lambda e, dvc=dvc, ps=ps: e.matmul(
                        ps[:, :512], onesbr[:, :],
                        brophalf[:, dvc * 512:(dvc + 1) * 512],
                        start=True, stop=False),
                        wait=([("a", evac[par])] if evac[par] else []) + [("qb1g", bb0)])
                for k in range(ND):
                    w = []
                    if k == 0 and variant != "l1":
                        w = [("a", evac[par])] if evac[par] else []
                        w.append((qwv, wv_ld))
                    p.t(lambda e, c=c, dvc=dvc, k=k, ps=ps: e.matmul(
                        ps[:, :512],
                        x_fm[:, k * T + c * 128: k * T + (c + 1) * 128],
                        v_rhs(k, dvc),
                        start=(k == 0 and variant != "l1"),
                        stop=(k == ND - 1)), wait=w)
                tc = p.c("t")
                evac[par] = p.s(lambda e, c=c, dvc=dvc, ps=ps: e.activation(
                    out=v_tm[:, c * D + dvc * 512: c * D + (dvc + 1) * 512],
                    in_=ps[:, :512], func=AF.Copy), wait=[("t", tc)])
                p.pump(1)

        if variant == "l0":
            rt1 = xst[:, 0:T // 2]
            rt2 = xst[:, T // 2:T]
            prj_a = p.c("a")
            TH = T // 2
            for buf in (q_fm, k_fm):
                for tile in range(ND):
                    for hh in range(2):
                        b0 = tile * T + hh * TH
                        w0 = ([("a", prj_a), ("qrope", 32)]
                              if tile == 0 and hh == 0 and buf is q_fm else [])
                        p.v(lambda e, b0=b0, buf=buf, hh=hh: e.tensor_mul(
                            out=rt1[:, :TH], in0=buf[:, b0:b0 + TH],
                            in1=ropec[:, hh * TH:(hh + 1) * TH]), wait=w0)
                        for g, sb in ((0, 32), (1, 0), (2, 96), (3, 64)):
                            p.v(lambda e, b0=b0, g=g, sb=sb, buf=buf, hh=hh: e.tensor_mul(
                                out=rt2[g * 32:(g + 1) * 32, :TH],
                                in0=buf[sb:sb + 32, b0:b0 + TH],
                                in1=ropes[sb:sb + 32, hh * TH:(hh + 1) * TH]))
                        p.v(lambda e, b0=b0, buf=buf: e.tensor_add(
                            out=buf[:, b0:b0 + TH], in0=rt1[:, :TH], in1=rt2[:, :TH]))
            rope_v = p.c("v")
            maskt = ropec
            mask_ld = 0
            for i in range(4):
                mask_ld = p.ld(maskt[:, i * 512:(i + 1) * 512],
                               masks_d[i * 128:(i + 1) * 128, :],
                               wait=([("v", rope_v)] if i == 0 else []),
                               q="qmk")
        else:
            rope_v = 0
            mask_ld = 0

        proj_a = p.c("a")
        uo = p.uid()
        if variant == "l1":
            def wob_sl(k, nn):
                return wqkv[:, k * 3 * D + nn * 512: k * 3 * D + (nn + 1) * 512]
            for k in range(ND):
                p.ldg(wqkv[:, k * 3 * D:k * 3 * D + D],
                      wo_d[k * 128:(k + 1) * 128, :],
                      wait=([("a", proj_a), ("v", p.c("v"))] if k == 0 else []),
                      q="qwo_%d" % uo)
        else:
            def wob_sl(k, nn):
                return wob[:, k * D + nn * 512: k * D + (nn + 1) * 512]
            for k in range(ND):
                p.ldg(wob[:, k * D:(k + 1) * D], wo_d[k * 128:(k + 1) * 128, :],
                      wait=([("a", proj_a), ("v", p.c("v"))] if k == 0 else []),
                      q="qwo_%d" % uo)
        wo_ld = p.c("qwo_%d" % uo)
        wo_q = "qwo_%d" % uo

        e_cons = {0: 0, 1: 0, 2: 0, 3: 0}
        last_exp = {0: 0, 1: 0, 2: 0, 3: 0}
        if p.bgq:
            ps_rot = (ps_s, ps_s2)
            et_rot = (et0, et1)
        else:
            ps_rot = (ps_s, ps_s2, ps_y, ps_y2)
            et_rot = (et0, et1, et2, et3)
        rot_n = len(ps_rot)
        pv_ev = [0]
        dv2_ev = [0]
        for h in range(NH):
            qrow = h // 2
            r0 = 64 * (h % 2)
            for c4 in range(4):
                jmax = 4 * c4 + 4
                first_s = True
                def emit_denpv(j, et, ev):
                    wden = [("a", ev)]
                    wpv = []
                    if j == 0:
                        if dv2_ev[0]:
                            wden.append(("v", dv2_ev[0]))
                        if pv_ev[0]:
                            wpv.append(("v", pv_ev[0]))
                    p.t(lambda e, et=et, j=j, jmax=jmax: e.matmul(
                        ps_den[:, :512], onescb[:, :], et[:, :512],
                        start=(j == 0), stop=(j == jmax - 1)), wait=wden)
                    p.t(lambda e, h=h, j=j, et=et, jmax=jmax: e.matmul(
                        ps_pv[:, :512],
                        v_tm[:, j * D + h * 64: j * D + (h + 1) * 64],
                        et[:, :512],
                        start=(j == 0), stop=(j == jmax - 1)), wait=wpv)
                    e_cons[j % rot_n] = p.c("t")

                pend = None
                for j in range(jmax):
                    par = j % rot_n
                    ps = ps_rot[par]
                    et = et_rot[par]
                    w = []
                    if first_s:
                        w = [("v", rope_v), ("a", proj_a)] if rope_v else [("a", proj_a)]
                        if mask_ld:
                            w.append(("qmk", mask_ld))
                        first_s = False
                    if last_exp[par]:
                        w.append(("a", last_exp[par]))
                    di = j - 4 * c4
                    if di >= 0:
                        p.t(lambda e, ps=ps, di=di: e.matmul(
                            ps[:, :512], identb[:],
                            maskt[:, di * 512:(di + 1) * 512],
                            start=True, stop=False), wait=w)
                        w = []
                    p.t(lambda e, h=h, c4=c4, j=j, ps=ps, qrow=qrow, r0=r0,
                        di=di: e.matmul(
                        ps[:, :512],
                        k_fm[r0:r0 + 64, qrow * T + j * 128: qrow * T + (j + 1) * 128],
                        q_fm[r0:r0 + 64, qrow * T + c4 * 512: qrow * T + (c4 + 1) * 512],
                        start=(di < 0), stop=True), wait=w)
                    tc = p.c("t")
                    ev = p.s(lambda e, ps=ps, et=et: e.activation(
                        out=et[:, :512], in_=ps[:, :512], func=AF.Exp,
                        scale=0.125),
                        wait=[("t", tc)] + ([("t", e_cons[par])] if e_cons[par] else []))
                    last_exp[par] = ev
                    if pend is not None:
                        emit_denpv(*pend)
                    pend = (j, et, ev)
                emit_denpv(*pend)
                tden = p.c("t")
                dv2 = p.v(lambda e: e.reciprocal(
                    out=den[0:1, :512], in_=ps_den[0:1, :512]),
                    wait=[("t", tden)])
                dv2_ev[0] = dv2
                tb = p.t(lambda e: e.matmul(
                    ps_bc[:, :512], onesr[0:1, 0:64], den[0:1, :512],
                    start=True, stop=True), wait=[("v", dv2)])
                bcp = p.s(lambda e: e.activation(
                    out=rbc[:, :512], in_=ps_bc[:64, :512], func=AF.Copy),
                    wait=[("t", tb)])
                pv_ev[0] = p.v(lambda e, h=h, c4=c4, qrow=qrow, r0=r0: e.tensor_mul(
                    out=attn_fm[r0:r0 + 64, qrow * T + c4 * 512: qrow * T + (c4 + 1) * 512],
                    in0=ps_pv[:64, :512], in1=rbc[:64, :512]),
                    wait=[("a", bcp)])
                p.pump(2)
        p.flush_bg()
        attn_v = p.c("v")

        last_store = [0]
        y_evac = {0: 0, 1: 0}
        for c in range(NT):
            qrn = "qr0"
            mx = (p.c("v") if variant == "l1" else max(y_evac[0], y_evac[1]))
            rq = p.ld(xres[:, :D],
                      x_in[c * 128:(c + 1) * 128, :],
                      wait=([("v", mx)] if mx else []) +
                           (p.dma_totals() if c == 0 else []),
                      q=qrn)
            for nn in range(2):
                ps = ps_y if nn == 0 else ps_y2
                if variant == "l1":
                    p.t(lambda e, nn=nn, ps=ps: e.matmul(
                        ps[:, :512], onesbr[:, :],
                        bprowb[:, nn * 512:(nn + 1) * 512],
                        start=True, stop=False),
                        wait=([("v", max(attn_v, y_evac[nn])), (wo_q, wo_ld), ("qb2", bb1)]
                              if (c == 0 or y_evac[nn]) else []))
                for k in range(ND):
                    w = []
                    if k == 0 and variant != "l1":
                        w = [("v", max(attn_v, y_evac[nn])), (wo_q, wo_ld)]
                    p.t(lambda e, c=c, nn=nn, k=k, ps=ps: e.matmul(
                        ps[:, :512],
                        attn_fm[:, k * T + c * 128: k * T + (c + 1) * 128],
                        wob_sl(k, nn),
                        start=(k == 0 and variant != "l1"),
                        stop=(k == ND - 1)), wait=w)
                tc = p.c("t")
                y_evac[nn] = p.v(lambda e, nn=nn, ps=ps: e.tensor_add(
                    out=pre[:, nn * 512:(nn + 1) * 512], in0=ps[:, :512],
                    in1=xres[:, nn * 512:(nn + 1) * 512]),
                    wait=[("t", tc), (qrn, rq)])
            emit_ln_tile(p, c, pre[:, :D], x_out, scr, stats, cst[:, 0:1],
                         wait=[("v", p.c("v"))], last_store=last_store)


def emit_l0attn(p, dtf, x_in, x_out):
    return emit_attn(p, dtf, x_in, x_out, "l0")


def emit_l1attn(p, dtf, x_in, x_out):
    return emit_attn(p, dtf, x_in, x_out, "l1")


def emit_l2attn(p, dtf, x_in, x_out):
    return emit_attn(p, dtf, x_in, x_out, "l2")


def _standalone(block):
    nc = bass.Bass("TRN2", target_bir_lowering=False)
    octx = ExitStack()
    with octx:
        p = Prog(nc, octx)

        def dtf(name, shape, dtype, kind="ExternalInput"):
            return nc.dram_tensor(name, shape, dtype, kind=kind)

        x_in = dtf("x_in", [T, D], F32).ap()
        x_out = dtf("x_out", [T, D], F32, kind="ExternalOutput").ap()
        EMITTERS[block](p, dtf, x_in, x_out)
        p.do("scalar", lambda e: e.wait_ge(p.sems["qa"], p.c("qa")))
        p.finalize()
    return nc


def build_l0attn():
    return _standalone("l0attn")


def build_l1attn():
    return _standalone("l1attn")


def build_l2attn():
    return _standalone("l2attn")


def emit_l3attn(p, dtf, x_in, x_out):
    if True:
        nc = p.nc
        ctx = p.ctx
        ident = dtf("ident", [128, 128], BF16, kind="ExternalInput").ap()
        ones_row = dtf("ones_row", [1, 128], F32, kind="ExternalInput").ap()
        trimask_d = dtf("trimaskn", [128, 128], BF16, kind="ExternalInput").ap()
        wq_d = dtf("wq", [D, D], F32, kind="ExternalInput").ap()
        wk_d = dtf("wk", [D, D], F32, kind="ExternalInput").ap()
        wv_d = dtf("wv", [D, D], F32, kind="ExternalInput").ap()
        wo_d = dtf("wo", [D, D], F32, kind="ExternalInput").ap()
        aq_d = dtf("aqt", [T, D], F32, kind="ExternalInput").ap()
        bq_d = dtf("bqt", [T, D], F32, kind="ExternalInput").ap()
        ck_d = dtf("ckt", [T, D], F32, kind="ExternalInput").ap()
        sk_d = dtf("skt", [T, D], F32, kind="ExternalInput").ap()

        identb = p.sbuf("identb", [128, 128], BF16)
        onesr = p.sbuf("onesr", [1, 128], F32)
        trimaskb = p.sbuf("trimaskb", [128, 128], BF16)
        cst = p.sbuf("cst", [128, 4], F32)
        p.ld(identb[:], ident)
        p.ld(onesr[:], ones_row)
        p.ld(trimaskb[:], trimask_d)
        c0 = p.c("qs")
        p.v(lambda e: e.memset(cst[:, 0:1], 1e-5), wait=[("qs", c0)])

        x_fm = p.sbuf("x_fm", [128, ND * T], BF16)
        attn_fm = x_fm
        qh_fm = p.sbuf("qh_fm", [128, ND * T], BF16)
        kh_fm = p.sbuf("kh_fm", [128, ND * T], BF16)
        v_tm = p.sbuf("v_tm", [128, NT * D], BF16)
        wbuf = p.sbuf("wbuf", [128, ND * D], BF16)
        qk_tm = p.sbuf("qk_tm", [128, 2 * D], BF16)
        eo = p.sbuf("eo", [128, 4 * T], BF16)
        tabs = eo[:, 0:4 * D]
        eaL = [p.sbuf("ea0", [128, T], BF16), p.sbuf("ea1", [128, T], BF16)]
        ebL = [p.sbuf("eb0", [128, T], BF16), p.sbuf("eb1", [128, T], BF16)]
        ptL = [p.sbuf("pt0", [128, NT * 128], BF16),
               p.sbuf("pt1", [128, NT * 128], BF16)]
        m8L = [p.sbuf("m80", [128, 8], F32), p.sbuf("m81", [128, 8], F32)]
        selvL = [p.sbuf("sv0", [128, 8], F32), p.sbuf("sv1", [128, 8], F32)]
        xst = p.sbuf("xst", [128, 2 * D], BF16)
        rt = xst
        xres = p.sbuf("xres", [128, D], F32)
        pre = xres
        scr = p.sbuf("scr", [128, D], F32)
        stats = p.sbuf("stats", [128, 16], F32)

        ps_s = p.psum("ps_s", [128, 512], F32)
        ps_s2 = p.psum("ps_s2", [128, 512], F32)
        ps_pvL = [p.psum("ps_pv", [64, 512], F32), p.psum("ps_pv2", [64, 512], F32)]
        ps_trL = [p.psum("ps_tr", [128, 512], BF16),
                  p.psum("ps_tr2", [128, 512], BF16)]
        ps_tr = ps_trL[0]
        ps_y = p.psum("ps_y", [128, 512], F32)
        ps_y2 = p.psum("ps_y2", [128, 512], F32)

        emit_stream_x_fm(p, x_in, x_fm, xst, identb,
                         wait0=[("qs", p.c("qs"))])
        xfm_a = p.c("a")

        uw = p.uid()
        wq_ld = {}

        def load_w(wsrc, tag):
            qn = "qw_%s_%d" % (tag, uw)
            for k in range(ND):
                p.ldg(wbuf[:, k * D:(k + 1) * D], wsrc[k * 128:(k + 1) * 128, :],
                      wait=([("t", p.c("t"))] if k == 0 else []), q=qn)
            return qn, p.c(qn)

        evac = {0: 0, 1: 0}
        for which, (wsrc, dst, tabA, tabB) in enumerate(
                ((wq_d, qh_fm, aq_d, bq_d), (wk_d, kh_fm, ck_d, sk_d))):
            qn, wl = load_w(wsrc, "qk%d" % which)
            ut = p.uid()
            for c in range(NT):
                tb = tabs[:, (c % 2) * 2 * D:((c % 2) + 1) * 2 * D]
                qnt = "qt%d_%d" % (c % 2, ut)
                wtb = [("v", p.c("v"))] if c >= 2 else []
                qt1 = p.ldg(tb[:, 0:D], tabA[c * 128:(c + 1) * 128, :], wait=wtb, q=qnt)
                qt2 = p.ldg(tb[:, D:2 * D], tabB[c * 128:(c + 1) * 128, :], q=qnt)
                par = c % 2
                st = qk_tm[:, par * D:(par + 1) * D]
                for oc2 in range(2):
                    ps = ps_s if oc2 == 0 else ps_s2
                    for k in range(ND):
                        w = []
                        if k == 0:
                            w = [(qn, wl), ("v", evac[oc2] if False else p.c("v"))]
                            if c == 0 and which == 0:
                                w.append(("a", xfm_a))
                        p.t(lambda e, c=c, oc2=oc2, k=k, ps=ps: e.matmul(
                            ps[:, :512],
                            x_fm[:, k * T + c * 128: k * T + (c + 1) * 128],
                            wbuf[:, k * D + oc2 * 512: k * D + (oc2 + 1) * 512],
                            start=(k == 0), stop=(k == ND - 1)), wait=w)
                    tc = p.c("t")
                    p.s(lambda e, oc2=oc2, st=st, ps=ps: e.activation(
                        out=st[:, oc2 * 512:(oc2 + 1) * 512], in_=ps[:, :512],
                        func=AF.Copy), wait=[("t", tc)])
                ac = p.c("a")
                p.v(lambda e, st=st, tb=tb: e.tensor_mul(
                    out=rt[:, 0:D], in0=st[:, 0:D], in1=tb[:, 0:D]),
                    wait=[("a", ac), (qnt, qt2)])
                p.v(lambda e, st=st, tb=tb: e.tensor_mul(
                    out=rt[:, D:2 * D:2], in0=st[:, 1:D:2], in1=tb[:, D:2 * D:2]))
                p.v(lambda e, st=st, tb=tb: e.tensor_mul(
                    out=rt[:, D + 1:2 * D:2], in0=st[:, 0:D:2], in1=tb[:, D + 1:2 * D:2]))
                vr = p.v(lambda e, st=st: e.tensor_add(
                    out=st[:, 0:D], in0=rt[:, 0:D], in1=rt[:, D:2 * D]))
                for f0 in range(0, ND, 4):
                    wv = [("v", vr), ("a", p.c("a"))]
                    for kk in range(4):
                        p.t(lambda e, f0=f0, kk=kk, st=st: e.transpose(
                            ps_tr[:, kk * 128:(kk + 1) * 128],
                            st[:, (f0 + kk) * 128:(f0 + kk + 1) * 128],
                            identb[:]), wait=(wv if kk == 0 else []))
                    tc = p.c("t")
                    for kk in range(4):
                        p.s(lambda e, f0=f0, kk=kk, c=c, dst=dst: e.activation(
                            out=dst[:, (f0 + kk) * T + c * 128:(f0 + kk) * T + (c + 1) * 128],
                            in_=ps_tr[:, kk * 128:(kk + 1) * 128], func=AF.Copy),
                            wait=[("t", tc)])
        qn, wl = load_w(wv_d, "v")
        for c in range(NT):
            for dvc in range(2):
                par = dvc % 2
                ps = ps_s if par == 0 else ps_s2
                for k in range(ND):
                    w = []
                    if k == 0:
                        w = [(qn, wl), ("a", evac[par])] if evac[par] else [(qn, wl)]
                    p.t(lambda e, c=c, dvc=dvc, k=k, ps=ps: e.matmul(
                        ps[:, :512],
                        x_fm[:, k * T + c * 128: k * T + (c + 1) * 128],
                        wbuf[:, k * D + dvc * 512: k * D + (dvc + 1) * 512],
                        start=(k == 0), stop=(k == ND - 1)), wait=w)
                tc = p.c("t")
                evac[par] = p.s(lambda e, c=c, dvc=dvc, ps=ps: e.activation(
                    out=v_tm[:, c * D + dvc * 512: c * D + (dvc + 1) * 512],
                    in_=ps[:, :512], func=AF.Copy), wait=[("t", tc)])
        proj_a = p.c("a")
        qn_o, wo_ld = None, 0
        qn_o = "qw_o_%d" % uw
        for k in range(ND):
            p.ldg(wbuf[:, k * D:(k + 1) * D], wo_d[k * 128:(k + 1) * 128, :],
                  wait=([("t", p.c("t")), ("a", proj_a)] if k == 0 else []), q=qn_o)
        wo_ld = p.c(qn_o)

        lexp = {0: 0, 1: 0}
        e_free = {0: 0, 1: 0, 2: 0, 3: 0}
        mask_done = {}
        pv_done = {0: 0, 1: 0}
        ea_rd = {0: 0, 1: 0}
        last = {0: 0, 1: 0}
        tiles = [(h, c) for h in range(NH) for c in range(NT)]

        def emit_scores(i):
            h, c = tiles[i]
            qrow = h // 2
            r0 = 64 * (h % 2)
            par = i % 4
            eo_t = eo[:, par * T:par * T + T]
            ncc = c // 4 + 1
            Wv = 128 * (c + 1)
            for cc in range(ncc):
                wcc = min(512, Wv - cc * 512)
                ps = ps_s if cc % 2 == 0 else ps_s2
                w = [("a", max(proj_a, lexp[cc % 2]))]
                if cc == 0 and e_free[par]:
                    w.append(("v", e_free[par]))
                if cc == ncc - 1:
                    wd = wcc - 128
                    if wd:
                        p.t(lambda e, h=h, c=c, cc=cc, ps=ps, qrow=qrow, r0=r0, wd=wd: e.matmul(
                            ps[:, :wd],
                            qh_fm[r0:r0 + 64, qrow * T + c * 128: qrow * T + (c + 1) * 128],
                            kh_fm[r0:r0 + 64, qrow * T + cc * 512: qrow * T + cc * 512 + wd],
                            start=True, stop=True), wait=w)
                    p.t(lambda e, ps=ps, wd=wd, wcc=wcc: e.matmul(
                        ps[:, wd:wcc], identb[:], trimaskb[:],
                        start=True, stop=False), wait=([] if wd else w))
                    p.t(lambda e, h=h, c=c, cc=cc, ps=ps, qrow=qrow, r0=r0, wd=wd, wcc=wcc: e.matmul(
                        ps[:, wd:wcc],
                        qh_fm[r0:r0 + 64, qrow * T + c * 128: qrow * T + (c + 1) * 128],
                        kh_fm[r0:r0 + 64, qrow * T + cc * 512 + wd: qrow * T + cc * 512 + wcc],
                        start=False, stop=True))
                else:
                    p.t(lambda e, h=h, c=c, cc=cc, ps=ps, qrow=qrow, r0=r0, wcc=wcc: e.matmul(
                        ps[:, :wcc],
                        qh_fm[r0:r0 + 64, qrow * T + c * 128: qrow * T + (c + 1) * 128],
                        kh_fm[r0:r0 + 64, qrow * T + cc * 512: qrow * T + cc * 512 + wcc],
                        start=True, stop=True), wait=w)
                tc = p.c("t")
                we = [("t", tc)]
                if cc < 2 and e_free[par]:
                    we.append(("v", e_free[par]))
                ev = p.s(lambda e, cc=cc, ps=ps, eo_t=eo_t, wcc=wcc: e.activation(
                    out=eo_t[:, cc * 512:cc * 512 + wcc], in_=ps[:, :wcc],
                    func=AF.Exp, scale=0.125), wait=we)
                lexp[cc % 2] = ev
                if cc == ncc - 1:
                    mask_done[i] = ("a", ev)

        def vop(L, fn, extra=()):
            w = ([("v", last[L])] if last[L] else []) + list(extra)
            last[L] = p.do("vector", fn, wait=w, inc="v", drain=False)
            return last[L]

        def emit_select_pair(i0):
            ids = (i0, i0 + 1)
            hs = [tiles[ii][0] for ii in ids]
            cs = [tiles[ii][1] for ii in ids]
            Ws = [128 * (c + 1) for c in cs]
            ets = [eo[:, (ii % 4) * T:(ii % 4) * T + T] for ii in ids]
            curs = [ets[0], ets[1]]
            for rnd in range(13):
                for L in (0, 1):
                    cur = curs[L]
                    vop(L, lambda e, cur=cur, W=Ws[L], m8=m8L[L]: e.max(
                        out=m8[:, :8], in_=cur[:, :W]),
                        extra=([mask_done[ids[L]]] if rnd == 0 else []))
                for L in (0, 1):
                    cur = curs[L]
                    nxt = eaL[L] if cur is ebL[L] else ebL[L]
                    extra = []
                    if rnd == 1 and ea_rd[L]:
                        extra = [("t", ea_rd[L])]
                    vop(L, lambda e, cur=cur, nxt=nxt, W=Ws[L], m8=m8L[L]:
                        e.match_replace(
                            out=nxt[:, :W], in_to_replace=m8[:, :8],
                            in_values=cur[:, :W], imm_value=-1.0), extra=extra)
                    curs[L] = nxt
            for L in (0, 1):
                e_free[ids[L] % 4] = vop(
                    L, lambda e, W=Ws[L], eo_t=ets[L], ea=eaL[L], m8=m8L[L],
                    selv=selvL[L]: e.scalar_tensor_tensor(
                        out=ea[:, :W], in0=eo_t[:, :W], scalar=m8[:, 3:4],
                        in1=eo_t[:, :W], op0=ALU.is_ge, op1=ALU.mult,
                        accum_out=selv[:, 1:2]))
            for L in (0, 1):
                vop(L, lambda e, selv=selvL[L]: e.reciprocal(
                    out=selv[:, 2:3], in_=selv[:, 1:2]))
            norms = [0, 0]
            for L in (0, 1):
                norms[L] = p.s(lambda e, W=Ws[L], ea=eaL[L], selv=selvL[L]:
                               e.activation(
                                   out=ea[:, :W], in_=ea[:, :W],
                                   func=AF.Identity, scale=selv[:, 2:3]),
                               wait=[("v", last[L])])
            for L in (0, 1):
                h, c, W = hs[L], cs[L], Ws[L]
                qrow = h // 2
                r0 = 64 * (h % 2)
                ntr = W // 128
                for f0 in range(0, ntr, 4):
                    g = min(4, ntr - f0)
                    wv = [("a", p.c("a"))]
                    for kk in range(g):
                        p.t(lambda e, f0=f0, kk=kk, ea=eaL[L], ps=ps_trL[L]:
                            e.transpose(
                                ps[:, kk * 128:(kk + 1) * 128],
                                ea[:, (f0 + kk) * 128:(f0 + kk + 1) * 128],
                                identb[:]), wait=(wv if kk == 0 else []))
                    tc = p.c("t")
                    for kk in range(g):
                        p.s(lambda e, f0=f0, kk=kk, pt=ptL[L], ps=ps_trL[L]:
                            e.activation(
                                out=pt[:, (f0 + kk) * 128:(f0 + kk + 1) * 128],
                                in_=ps[:, kk * 128:(kk + 1) * 128],
                                func=AF.Copy), wait=[("t", tc)])
                ea_rd[L] = p.c("t")
                tq_a = p.c("a")
                nj = c + 1
                for j in range(nj):
                    w = [("a", tq_a)] if j == 0 else []
                    if j == 0 and pv_done[L]:
                        w.append(pv_done[L])
                    p.t(lambda e, j=j, h=h, nj=nj, ps=ps_pvL[L], pt=ptL[L]:
                        e.matmul(
                            ps[:64, :128],
                            v_tm[:, j * D + h * 64: j * D + (h + 1) * 64],
                            pt[:, j * 128:(j + 1) * 128],
                            start=(j == 0), stop=(j == nj - 1)), wait=w)
                tc = p.c("t")
                pv_done[L] = ("a", p.s(
                    lambda e, h=h, c=c, qrow=qrow, r0=r0, ps=ps_pvL[L]:
                    e.activation(
                        out=attn_fm[r0:r0 + 64,
                                    qrow * T + c * 128: qrow * T + (c + 1) * 128],
                        in_=ps[:64, :128], func=AF.Copy), wait=[("t", tc)]))

        n_tiles = len(tiles)
        for base in range(0, n_tiles + 2, 2):
            if base < n_tiles:
                emit_scores(base)
                emit_scores(base + 1)
            if base >= 2:
                emit_select_pair(base - 2)
        attn_v = p.c("v")
        attn_a = p.c("a")

        last_store = [0]
        y_evac = {0: 0, 1: 0}
        for c in range(NT):
            qrn = "qr0"
            rq = p.ld(xres[:, :D], x_in[c * 128:(c + 1) * 128, :],
                      wait=[("v", p.c("v"))] + (p.dma_totals() if c == 0 else []),
                      q=qrn)
            for nn in range(2):
                ps = ps_y if nn == 0 else ps_y2
                for k in range(ND):
                    w = []
                    if k == 0:
                        w = [("v", max(attn_v, y_evac[nn])), (qn_o, wo_ld),
                             ("a", attn_a)]
                    p.t(lambda e, c=c, nn=nn, k=k, ps=ps: e.matmul(
                        ps[:, :512],
                        attn_fm[:, k * T + c * 128: k * T + (c + 1) * 128],
                        wbuf[:, k * D + nn * 512: k * D + (nn + 1) * 512],
                        start=(k == 0), stop=(k == ND - 1)), wait=w)
                tc = p.c("t")
                y_evac[nn] = p.v(lambda e, nn=nn, ps=ps: e.tensor_add(
                    out=pre[:, nn * 512:(nn + 1) * 512], in0=ps[:, :512],
                    in1=xres[:, nn * 512:(nn + 1) * 512]),
                    wait=[("t", tc), (qrn, rq)])
            emit_ln_tile(p, c, pre[:, :D], x_out, scr, stats, cst[:, 0:1],
                         wait=[("v", p.c("v"))], last_store=last_store)


EMITTERS = {"l0attn": emit_l0attn, "l1attn": emit_l1attn,
            "l2attn": emit_l2attn, "l3attn": emit_l3attn,
            "l0ffn": emit_l0ffn, "l1ffn": emit_l1ffn,
            "l2ffn": emit_l2ffn, "l3ffn": emit_l3ffn}


def build_l0ffn():
    return _standalone("l0ffn")


def build_l1ffn():
    return _standalone("l1ffn")


def build_l2ffn():
    return _standalone("l2ffn")


def build_l3ffn():
    return _standalone("l3ffn")


def build_l3attn():
    return _standalone("l3attn")


_PREP_HOST = {"l0attn": ("l0ffn", _declare_l0prep, _init_l0prep),
              "l2attn": ("l2ffn", _declare_l2prep, _init_l2prep)}


def build_chain(blocks):
    nc = bass.Bass("TRN2", target_bir_lowering=False)
    octx = ExitStack()
    with octx:
        p = Prog(nc, octx)
        x_ext = nc.dram_tensor("x_in", [T, D], F32, kind="ExternalInput").ap()
        y_ext = nc.dram_tensor("x_out", [T, D], F32, kind="ExternalOutput").ap()
        chain = [nc.dram_tensor("xc%d" % i, [T, D], F32, kind="Internal").ap()
                 for i in range(len(blocks) - 1)]
        xs = [x_ext] + chain
        ys = chain + [y_ext]
        ext_store = {}
        for bi, blk in enumerate(blocks):
            px = blk + "_"

            def dtf(name, shape, dtype, kind="ExternalInput", px=px):
                return nc.dram_tensor(px + name, shape, dtype, kind=kind)

            inner = ExitStack()
            p.ctx = inner
            p.prefix = px
            p.buf.clear()
            host = _PREP_HOST.get(blk)
            if host and host[0] in blocks:
                hf, decl, init = host

                def dtfp(name, shape, dtype, kind="ExternalInput", hf=hf):
                    return nc.dram_tensor(hf + "_" + name, shape, dtype,
                                          kind=kind)
                handles = decl(dtfp)
                ext_store[hf] = handles
                p.bgq_init = (lambda h=handles, init=init: init(p, h))
            ext = ext_store.get(blk)
            if ext is not None:
                EMITTERS[blk](p, dtf, xs[bi], ys[bi], ext=ext)
            else:
                EMITTERS[blk](p, dtf, xs[bi], ys[bi])
            assert not p.bgq and p.bgq_init is None, blk
            p.block_barrier()
            inner.close()
            p.ctx = octx
        p.prefix = ""
        p.do("scalar", lambda e: e.wait_ge(p.sems["qa"], p.c("qa")))
        p.finalize()
    return nc


def build_fused():
    return build_chain(_BLOCK_ORDER)


BUILDERS = {"l1ffn": build_l1ffn, "l3ffn": build_l3ffn, "l0ffn": build_l0ffn,
            "l2ffn": build_l2ffn, "l0attn": build_l0attn, "l1attn": build_l1attn,
            "l2attn": build_l2attn, "l3attn": build_l3attn}


def make_in_map(block, inputs, x_shard):
    m = {"x_in": np.ascontiguousarray(x_shard.astype(np.float32)),
         "ident": _IDENT_BF, "ones_row": _ONES_ROW}
    if block in ("l3ffn",):
        m["ones_bf"] = _ONES_BF
    if block in ("l0attn", "l1attn", "l2attn"):
        m["ones_bf"] = _ONES_BF
        m["onescb"] = _ONES_CB
        m["masks"] = _MASKS_BIAS_T
    if block == "l0attn":
        m["ropec"] = _ROPE_C
        m["ropes"] = _ROPE_S
        m["wqT"] = np.ascontiguousarray(inputs["L0_attn_Wq"].astype(np.float32).T)
        m["wkT"] = np.ascontiguousarray(inputs["L0_attn_Wk"].astype(np.float32).T)
        m["wvT"] = np.ascontiguousarray(inputs["L0_attn_Wv"].astype(np.float32).T)
        m["woT"] = np.ascontiguousarray(inputs["L0_attn_Wo"].astype(np.float32).T)
    if block == "l2attn":
        m["wqT"] = np.ascontiguousarray(inputs["L2_attn_Wq"].astype(np.float32).T)
        m["wkT"] = np.ascontiguousarray(inputs["L2_attn_Wk"].astype(np.float32).T)
        m["wvT"] = np.ascontiguousarray(inputs["L2_attn_Wv"].astype(np.float32).T)
        m["woT"] = np.ascontiguousarray(inputs["L2_attn_Wo"].astype(np.float32).T)
    if block == "l3attn":
        m["trimaskn"] = _TRIMASK_NEG
        aq, bq, ck, sk = _l3_tables_cached()
        m["aqt"], m["bqt"], m["ckt"], m["skt"] = aq, bq, ck, sk
        m["wq"] = np.ascontiguousarray(inputs["L3_attn_Wq"].astype(np.float32))
        m["wk"] = np.ascontiguousarray(inputs["L3_attn_Wk"].astype(np.float32))
        m["wv"] = np.ascontiguousarray(inputs["L3_attn_Wv"].astype(np.float32))
        m["wo"] = np.ascontiguousarray(inputs["L3_attn_Wo"].astype(np.float32))
    if block == "l1attn":
        m["wqkv"] = np.ascontiguousarray(inputs["L1_attn_w_attn"].astype(np.float32))
        m["wo"] = np.ascontiguousarray(inputs["L1_attn_w_proj"].astype(np.float32))
        b = inputs["L1_attn_b_attn"].astype(np.float32)
        bf = np.zeros((128, 24), np.float32)
        bf[:, 0:8] = b[0:1024].reshape(8, 128).T
        bf[:, 8:16] = b[1024:2048].reshape(8, 128).T
        m["battn"] = bf
        m["battnr"] = np.ascontiguousarray(b[2048:3072]).reshape(1, D)
        m["bprojr"] = inputs["L1_attn_b_proj"].astype(np.float32).reshape(1, D)
    if block == "l1ffn":
        m["w"] = np.ascontiguousarray(inputs["L1_ffn_weight"].astype(np.float32))
        m["b"] = np.ascontiguousarray(inputs["L1_ffn_bias"].astype(np.float32)).reshape(1, D)
    elif block == "l0ffn":
        m["identf"] = np.eye(128, dtype=np.float32)
        m["w1t"] = np.ascontiguousarray(inputs["L0_ffn_w1"].astype(np.float32).T)
        m["w2t"] = np.ascontiguousarray(inputs["L0_ffn_w2"].astype(np.float32).T)
        cw = inputs["L0_ffn_conv_w"].astype(np.float32).reshape(FF, 3)
        cwf = np.zeros((128, 3 * NF), np.float32)
        for tap in range(3):
            cwf[:, tap * NF:(tap + 1) * NF] = cw[:, tap].reshape(NF, 128).T
        m["cw"] = cwf
        m["cb"] = np.ascontiguousarray(
            inputs["L0_ffn_conv_b"].astype(np.float32).reshape(NF, 128).T)
    elif block == "l2ffn":
        m["wgt"] = np.ascontiguousarray(inputs["L2_ffn_w_gate"].astype(np.float32).T)
        m["wut"] = np.ascontiguousarray(inputs["L2_ffn_w_up"].astype(np.float32).T)
        m["wdt"] = np.ascontiguousarray(inputs["L2_ffn_w_down"].astype(np.float32).T)
    elif block == "l3ffn":
        m["w1t"] = np.ascontiguousarray(inputs["L3_ffn_W1"].astype(np.float32).T)
        m["b1f"] = np.ascontiguousarray(
            inputs["L3_ffn_b1"].astype(np.float32).reshape(NF, 128).T)
        m["w2t"] = np.ascontiguousarray(inputs["L3_ffn_W2"].astype(np.float32).T)
        m["b2r"] = np.ascontiguousarray(
            inputs["L3_ffn_b2"].astype(np.float32)).reshape(1, D)
    return m


def run_l1ffn(x_full, weight, bias):
    nc = build_l1ffn()
    xs = _shard_x(x_full)
    in_maps = []
    for i in range(8):
        in_maps.append({
            "x_in": xs[i],
            "w": np.ascontiguousarray(weight.astype(np.float32)),
            "b": np.ascontiguousarray(bias.astype(np.float32)).reshape(1, D),
            "ident": _IDENT_BF,
            "ones_row": _ONES_ROW,
        })
    res = run_bass_kernel_spmd(nc, in_maps, core_ids=list(range(8)))
    return np.stack([r["x_out"] for r in res.results], axis=0)


_BLOCK_ORDER = ["l0attn", "l0ffn", "l1attn", "l1ffn",
                "l2attn", "l2ffn", "l3attn", "l3ffn"]
_NC_CACHE = {}


def _get_nc(block):
    if block not in _NC_CACHE:
        _NC_CACHE[block] = BUILDERS[block]()
    return _NC_CACHE[block]


def make_fused_in_map(inputs, x_shard):
    m = {"x_in": np.ascontiguousarray(x_shard.astype(np.float32))}
    for blk in _BLOCK_ORDER:
        bm = make_in_map(blk, inputs, x_shard)
        for k, v in bm.items():
            if k == "x_in":
                continue
            m[blk + "_" + k] = v
    return m


def kernel(**inputs):
    x = np.asarray(inputs["L0_attn_x"], dtype=np.float32)
    if "fused" not in _NC_CACHE:
        _NC_CACHE["fused"] = build_fused()
    nc = _NC_CACHE["fused"]
    in_maps = [make_fused_in_map(inputs, x[i]) for i in range(8)]
    res = run_bass_kernel_spmd(nc, in_maps, core_ids=list(range(8)))
    return np.stack([r["x_out"] for r in res.results], axis=0)


def kernel_unfused(**inputs):
    x = np.asarray(inputs["L0_attn_x"], dtype=np.float32)
    shards = [np.ascontiguousarray(x[i]) for i in range(8)]
    for block in _BLOCK_ORDER:
        nc = _get_nc(block)
        in_maps = [make_in_map(block, inputs, shards[i]) for i in range(8)]
        res = run_bass_kernel_spmd(nc, in_maps, core_ids=list(range(8)))
        shards = [np.ascontiguousarray(r["x_out"]) for r in res.results]
    return np.stack(shards, axis=0)
